# version 25
# baseline (speedup 1.0000x reference)
from contextlib import ExitStack
import os
import numpy as np
import ml_dtypes
import concourse.bass as bass
import concourse.mybir as mybir
from concourse.bass_utils import run_bass_kernel_spmd

F32 = mybir.dt.float32
BF16 = mybir.dt.bfloat16
AF = mybir.ActivationFunctionType
ALU = mybir.AluOpType
AX = mybir.AxisListType

P = 128
D = 1024
NCH = 8
NT = 19
QT0, QT1 = 1, 17
CT0, CT1 = 2, 17
NH = 16
HD = 64

ENGS = ("sync", "vector", "scalar", "tensor", "gpsimd")


class Prog:
    def __init__(self, sems):
        self.sems = sems
        self.ops = {e: [] for e in ENGS}
        self.cnt = {}
        self.waited = {e: {} for e in ENGS}

    def wait(self, eng, ev):
        if ev is None:
            return
        if isinstance(ev, list):
            for x in ev:
                self.wait(eng, x)
            return
        sem, val, src = ev
        if self.waited[eng].get(sem, -1) >= val:
            return
        self.waited[eng][sem] = val
        self.ops[eng].append(("w", sem, val))

    def do(self, eng, fn, inc=None, n=1):
        if inc is None:
            self.ops[eng].append(("o", fn, None, 0))
            return None
        self.cnt[inc] = self.cnt.get(inc, 0) + n
        self.ops[eng].append(("o", fn, inc, n))
        return (inc, self.cnt[inc], eng)

    def later(self, *evs):
        best = {}
        for ev in evs:
            if ev is None:
                continue
            if isinstance(ev, list):
                for x in ev:
                    if x is not None and (x[0] not in best or best[x[0]][1] < x[1]):
                        best[x[0]] = x
                continue
            if ev[0] not in best or best[ev[0]][1] < ev[1]:
                best[ev[0]] = ev
        return list(best.values())


def build_program():
    nc = bass.Bass("TRN2", target_bir_lowering=False)

    x_d = nc.dram_tensor("x", [NT * P, D], F32, kind="ExternalInput")
    wqkv_d = nc.dram_tensor("wqkvT", [D, 3 * D], F32, kind="ExternalInput")
    wout_d = nc.dram_tensor("woutT", [D, D], F32, kind="ExternalInput")
    wr_d = nc.dram_tensor("wrT", [D, D], F32, kind="ExternalInput")
    wk_d = nc.dram_tensor("wkT", [D, D], F32, kind="ExternalInput")
    wv_d = nc.dram_tensor("wvT", [D, D], F32, kind="ExternalInput")
    gam_d = nc.dram_tensor("gamma_fm", [P, NCH], F32, kind="ExternalInput")
    bet_d = nc.dram_tensor("beta_fm", [P, NCH], F32, kind="ExternalInput")
    mur_d = nc.dram_tensor("mur_fm", [P, NCH], F32, kind="ExternalInput")
    muk_d = nc.dram_tensor("muk_fm", [P, NCH], F32, kind="ExternalInput")
    gate_d = nc.dram_tensor("gate", [P, 1], F32, kind="ExternalInput")
    id_d = nc.dram_tensor("ident", [P, P], BF16, kind="ExternalInput")
    mmid_d = nc.dram_tensor("mask_mid", [P, 384], BF16, kind="ExternalInput")
    mq2_d = nc.dram_tensor("mask_q2", [P, 384], BF16, kind="ExternalInput")
    mq17_d = nc.dram_tensor("mask_q17", [P, 384], BF16, kind="ExternalInput")
    out_d = nc.dram_tensor("out", [16 * P, D], F32, kind="ExternalOutput")

    with ExitStack() as ctx:
        e = ctx.enter_context

        wqkv = e(nc.sbuf_tensor("wqkv", [P, NCH * 3 * D], BF16))
        wout = e(nc.sbuf_tensor("wout", [P, NCH * D], BF16))
        wr = e(nc.sbuf_tensor("wr", [P, NCH * D], BF16))
        wk = e(nc.sbuf_tensor("wk", [P, NCH * D], BF16))
        wv = e(nc.sbuf_tensor("wv", [P, NCH * D], BF16))
        gam = e(nc.sbuf_tensor("gam", [P, NCH], F32))
        bet = e(nc.sbuf_tensor("bet", [P, NCH], F32))
        mur = e(nc.sbuf_tensor("mur", [P, NCH], F32))
        muk = e(nc.sbuf_tensor("muk", [P, NCH], F32))
        gate = e(nc.sbuf_tensor("gate_t", [P, 1], F32))
        ident = e(nc.sbuf_tensor("ident_t", [P, P], BF16))
        m_mid = e(nc.sbuf_tensor("m_mid", [P, 384], BF16))
        m_q2 = e(nc.sbuf_tensor("m_q2", [P, 384], BF16))
        m_q17 = e(nc.sbuf_tensor("m_q17", [P, 384], BF16))
        s_sb = e(nc.sbuf_tensor("s_sb", [P, D], BF16))
        consts = e(nc.sbuf_tensor("consts", [P, 10], F32))
        EPS, NHALF, INV_D, EIGHTH, ZERO, TWO, ONE, NEGONE, INV_SQD, NINV_D = range(10)

        x_in = [e(nc.sbuf_tensor(f"x_in{i}", [P, D], F32)) for i in range(3)]
        xnz = [e(nc.sbuf_tensor(f"xnz{i}", [P, D], BF16)) for i in range(2)]
        xn_fm = [e(nc.sbuf_tensor(f"xn_fm{i}", [P, D], BF16)) for i in range(2)]
        q_sb = [e(nc.sbuf_tensor(f"q_sb{i}", [P, D], BF16)) for i in range(2)]
        k_sb = [e(nc.sbuf_tensor(f"k_sb{i}", [P, D], BF16)) for i in range(4)]
        v_sb = [e(nc.sbuf_tensor(f"v_sb{i}", [P, D], BF16)) for i in range(4)]
        probs = [e(nc.sbuf_tensor(f"probs{i}", [P, 384], BF16)) for i in range(4)]
        probsT = e(nc.sbuf_tensor("probsT", [P, 768], BF16))
        outfm = e(nc.sbuf_tensor("outfm", [P, D], BF16))
        a2buf = e(nc.sbuf_tensor("a2buf", [P, D], F32))
        y_tm = [e(nc.sbuf_tensor(f"y_tm{i}", [P, D], F32)) for i in range(2)]
        ybf = e(nc.sbuf_tensor("ybf", [P, D], BF16))
        y_fm = [e(nc.sbuf_tensor(f"y_fm{i}", [P, NCH * 130], BF16)) for i in range(2)]
        delta = e(nc.sbuf_tensor("delta", [P, D], BF16))
        xr_fm = e(nc.sbuf_tensor("xr_fm", [P, D], BF16))
        xk_fm = e(nc.sbuf_tensor("xk_fm", [P, D], BF16))
        kact = e(nc.sbuf_tensor("kact", [P, D], BF16))
        krelu = delta
        sig = e(nc.sbuf_tensor("sig", [P, D], F32))
        ae_scr = e(nc.sbuf_tensor("ae_scr", [P, D], F32))
        out_sb = [e(nc.sbuf_tensor(f"out_sb{i}", [P, D], F32)) for i in range(2)]

        def slots(name, n=2):
            return [e(nc.sbuf_tensor(f"{name}{i}", [P, 1], F32)) for i in range(n)]
        sx1, e21, m1, p1s, r1 = slots("sx1"), slots("e21"), slots("m1"), slots("p1sl"), slots("r1")
        s2s, e22, m2, p2s, r2 = slots("s2s"), slots("e22"), slots("m2"), slots("p2s"), slots("r2")
        s3s, e23, m3, p3s, r3 = slots("s3s"), slots("e23"), slots("m3"), slots("p3s"), slots("r3")
        nm2s = slots("nm2s", 1)[0]
        lnvs = slots("lnvs", 1)[0]
        nmax2 = [e(nc.sbuf_tensor(f"nmax2_{i}", [P, 2], F32)) for i in range(2)]
        ssum = e(nc.sbuf_tensor("ssum", [P, NH], F32))
        rsum = e(nc.sbuf_tensor("rsum", [P, NH], F32))

        T0 = e(nc.psum_tensor("T0", [P, 1024], BF16))
        T3 = e(nc.psum_tensor("T3", [P, 1024], BF16))
        P12 = e(nc.psum_tensor("P12", [P, 1024], F32))
        P45 = e(nc.psum_tensor("P45", [P, 1024], F32))
        P67 = e(nc.psum_tensor("P67", [P, 1024], F32))

        sems = {}
        for s in ("s_c", "s_x0", "s_x1", "s_x2", "s_wq", "s_wo", "s_wr",
                  "s_wk", "s_wv", "s_st0", "s_st1",
                  "s_pe", "s_ve", "s_ae", "s_gp",
                  *[f"s_wq{i}" for i in range(NCH)]):
            sems[s] = e(nc.semaphore(s))

        pr = Prog(sems)

        def W(eng, *evs):
            for ev in evs:
                pr.wait(eng, ev)

        O = pr.do

        def C(j):
            return consts[:, j:j + 1]

        def dma(eng, sem, out_ap, in_ap):
            cnt = pr.cnt.get(sem, 0) + 16
            pr.cnt[sem] = cnt
            pr.ops[eng].append(
                ("o", lambda E, o=out_ap, i=in_ap: E.dma_start(o, i), sem, 16))
            return (sem, cnt, "_dma")

        ld_ev = {}
        ev = None
        for dst, src in ((gam, gam_d), (bet, bet_d), (mur, mur_d), (muk, muk_d),
                         (gate, gate_d), (ident, id_d), (m_mid, mmid_d),
                         (m_q2, mq2_d), (m_q17, mq17_d)):
            ev = dma("sync", "s_c", dst[:, :], src[:, :])
        ev_c = ev

        evm = None
        for j, val in ((EPS, 1e-5), (NHALF, -0.5), (INV_D, 1.0 / D),
                       (EIGHTH, 0.125), (ZERO, 0.0), (TWO, 2.0), (ONE, 1.0),
                       (NEGONE, -1.0), (INV_SQD, 1.0 / 32.0),
                       (NINV_D, -1.0 / D)):
            evm = O("vector", lambda E, jj=j, vv=val: E.memset(
                consts[:, jj:jj + 1], vv), "s_ve")
        ev_cv = evm

        ev_wq_c = []
        for c in range(NCH):
            ev_wq_c.append(dma("gpsimd", f"s_wq{c}",
                               wqkv[:, 3 * D * c:3 * D * (c + 1)],
                               wqkv_d[P * c:P * (c + 1), :]))
        ev_wq = ev_wq_c[-1]
        wdef = {}

        for t in range(3):
            ld_ev[t] = dma("sync", f"s_x{t % 3}", x_in[t % 3][:, :],
                           x_d[P * t:P * (t + 1), :])

        g = {
            "t0": None, "t3": None,
            "p12": None, "p45": None, "p67": None,
            "xn_fm": [None, None], "q": [None, None],
            "k": [None] * 4, "v": [None] * 4,
            "x_in": [None] * 3, "xnz": [None, None],
            "pm": [None] * 4, "probsT": [None, None],
            "a2buf": None, "y_tm": [None, None], "y_fm": [None, None],
            "out_sb": [None, None],
            "stat1": [None, None], "stat2": [None, None], "stat3": [None, None],
            "nmax": [None, None], "ssum": [None] * NH, "rsumg": None,
            "ybf": None, "outfm": None, "delta": None, "ae_scr": None,
            ("probs_buf", 0): None, ("probs_buf", 1): None,
            ("probs_buf", 2): None, ("probs_buf", 3): None,
            "xr": None, "xk": None, "krelu": None, "kact": None, "sig": None,
        }
        for par in (0, 1):
            for hp in (0, 1):
                g["sc", (par, hp)] = None
        kevac, vevac, qevac = {}, {}, {}
        scores_done, pv_done, yevac, a2ev = {}, {}, {}, {}
        xnev = {}

        def stats(src_ap, sumslot, e2slot, mslot, epslot, rslot, evsum, extra=None):
            W("scalar", evsum, ev_cv, extra, g["ae_scr"])
            evsq = O("scalar", lambda E, o=ae_scr[:, :], i=src_ap,
                     a=e2slot[:, :], s=C(INV_SQD): E.activation(
                o, i, AF.Square, scale=s, accum_out=a), "s_ae")
            W("scalar", evsq)
            ev1 = O("scalar", lambda E, o=epslot[:, :], i=e2slot[:, :],
                    b=C(EPS): E.activation(o, i, AF.Identity, bias=b), "s_ae")
            ev2 = O("scalar", lambda E, o=mslot[:, :], i=sumslot[:, :],
                    s=C(NINV_D): E.activation(o, i, AF.Copy, scale=s), "s_ae")
            W("scalar", ev2)
            ev3 = O("scalar", lambda E, o=nm2s[:, :], i=mslot[:, :]:
                    E.activation(o, i, AF.Square), "s_ae")
            W("scalar", ev1, ev3)
            ev4 = O("scalar", lambda E, o=lnvs[:, :], i=nm2s[:, :], s=C(NEGONE),
                    b=epslot[:, :]: E.activation(o, i, AF.Ln, scale=s, bias=b),
                    "s_ae")
            W("scalar", ev4)
            evr = O("scalar", lambda E, o=rslot[:, :], i=lnvs[:, :],
                    s=C(NHALF): E.activation(o, i, AF.Exp, scale=s), "s_ae")
            g["ae_scr"] = evsq
            return evsq, evr

        def A1_elem(t):
            i3, i2 = t % 3, t % 2
            xin = x_in[i3]
            W("vector", ld_ev[t], ev_cv, g["stat1"][i2])
            evsum = O("vector", lambda E, o=sx1[i2][:, :], i=xin[:, :]:
                      E.tensor_reduce(o, i, axis=AX.X, op=ALU.add), "s_ve")
            evsq, evr = stats(xin[:, :], sx1[i2], e21[i2], m1[i2], p1s[i2],
                              r1[i2], evsum, extra=pr.later(ld_ev[t], g["stat1"][i2]))
            W("vector", evr, g["xnz"][i2])
            evxnz = O("vector", lambda E, o=xnz[i2][:, :], i=xin[:, :],
                      a=m1[i2][:, :], b=r1[i2][:, :]: E.tensor_scalar(
                o, i, a, b, op0=ALU.add, op1=ALU.mult), "s_ve")
            g["stat1"][i2] = evxnz
            g["x_in"][i3] = pr.later(evxnz, evsq)
            return evxnz

        def A1_T(t, evxnz):
            i2 = t % 2
            W("tensor", evxnz, g["t0"], ev_c)
            evtp = None
            for c in range(NCH):
                evtp = O("tensor", lambda E, o=T0[:, P * c:P * (c + 1)],
                         i=xnz[i2][:, P * c:P * (c + 1)], d=ident[:, :]:
                         E.transpose(o, i, d), "s_pe" if c == NCH - 1 else None)
            g["xnz"][i2] = evtp
            W("scalar", evtp, ev_c, g["xn_fm"][i2])
            evxe = None
            for c in range(NCH):
                evxe = O("scalar", lambda E, o=xn_fm[i2][:, P * c:P * (c + 1)],
                         i=T0[:, P * c:P * (c + 1)], a=gam[:, c:c + 1],
                         b=bet[:, c:c + 1]: E.activation(
                    o, i, AF.Identity, scale=a, bias=b),
                    "s_ae" if c == NCH - 1 else None)
            g["t0"] = evxe
            xnev[t] = evxe
            return evxe

        def mm_q(t):
            i2 = t % 2
            W("tensor", xnev[t], g["p45"], g["q"][i2])
            last = None
            for r in range(NCH):
                for c in range(NCH):
                    W("tensor", ev_wq_c[c])
                    last = O("tensor", lambda E,
                             o=P45[:, P * r:P * (r + 1)],
                             l=wqkv[:, 3 * D * c + P * r:3 * D * c + P * (r + 1)],
                             rh=xn_fm[i2][:, P * c:P * (c + 1)],
                             st=(r % 4 == 0 and c == 0),
                             sp=(r % 4 == 3 and c == NCH - 1):
                             E.matmul(o, l, rh, start=st, stop=sp),
                             "s_pe" if (r == NCH - 1 and c == NCH - 1) else None)
            W("vector", last, g["q"][i2])
            evq = None
            for b in range(2):
                evq = O("vector", lambda E, o=q_sb[i2][:, 512 * b:512 * (b + 1)],
                        i=P45[:, 512 * b:512 * (b + 1)], s=C(EIGHTH):
                        E.tensor_scalar(o, i, s, None, op0=ALU.mult),
                        "s_ve" if b == 1 else None)
            g["p45"] = evq
            qevac[t] = evq
            g["xn_fm_q", t] = last
            return last

        def mm_k(t):
            i2, i4 = t % 2, t % 4
            W("tensor", xnev[t], g["p67"])
            last = None
            for r in range(NCH):
                for c in range(NCH):
                    W("tensor", ev_wq_c[c])
                    last = O("tensor", lambda E,
                             o=P67[:, P * r:P * (r + 1)],
                             l=wqkv[:, 3 * D * c + D + P * r:3 * D * c + D + P * (r + 1)],
                             rh=xn_fm[i2][:, P * c:P * (c + 1)],
                             st=(r % 4 == 0 and c == 0),
                             sp=(r % 4 == 3 and c == NCH - 1):
                             E.matmul(o, l, rh, start=st, stop=sp),
                             "s_pe" if (r == NCH - 1 and c == NCH - 1) else None)
            W("scalar", last, g["k"][i4])
            evke = None
            for b in range(2):
                evke = O("scalar", lambda E, o=k_sb[i4][:, 512 * b:512 * (b + 1)],
                         i=P67[:, 512 * b:512 * (b + 1)]:
                         E.activation(o, i, AF.Copy), "s_ae" if b == 1 else None)
            g["p67"] = evke
            g["sc", (1, 0)] = pr.later(g["sc", (1, 0)], evke)
            g["sc", (1, 1)] = pr.later(g["sc", (1, 1)], evke)
            kevac[t] = evke
            return last

        def mm_v(t):
            i2, i4 = t % 2, t % 4
            W("tensor", xnev[t], g["p12"])
            last = None
            for b in range(2):
                for c in range(NCH):
                    W("tensor", ev_wq_c[c])
                    last = O("tensor", lambda E,
                             o=P12[:, 512 * b:512 * (b + 1)],
                             l=xn_fm[i2][:, P * c:P * (c + 1)],
                             rh=wqkv[:, 3 * D * c + 2 * D + 512 * b:
                                     3 * D * c + 2 * D + 512 * (b + 1)],
                             st=(c == 0), sp=(c == NCH - 1):
                             E.matmul(o, l, rh, start=st, stop=sp),
                             "s_pe" if (b == 1 and c == NCH - 1) else None)
            W("vector", last, g["v"][i4])
            evve = None
            for b in range(2):
                evve = O("vector", lambda E, o=v_sb[i4][:, 512 * b:512 * (b + 1)],
                         i=P12[:, 512 * b:512 * (b + 1)]:
                         E.tensor_copy(o, i), "s_ve" if b == 1 else None)
            g["p12"] = evve
            g["sc", (0, 0)] = pr.later(g["sc", (0, 0)], evve)
            g["sc", (0, 1)] = pr.later(g["sc", (0, 1)], evve)
            vevac[t] = evve
            return last

        def ch_ev_sums(evd, n):
            evs = []
            for pp_ in range(n + 1):
                if pp_ in evd:
                    evs.extend([evd[pp_][0], evd[pp_][1]])
            return pr.later(*evs)

        def SC(qt, p, mask):
            bankT = P12 if p % 2 == 0 else P67
            evs = {}
            W("tensor", qevac[qt], kevac[qt - 1], kevac[qt], kevac[qt + 1],
              ev_c)
            W("tensor", g["sc", (p % 2, 0)], g["sc", (p % 2, 1)])
            i2 = qt % 2
            for hp in range(2):
                col0 = 512 * hp
                O("tensor", lambda E, o=bankT[:, col0:col0 + 2 * P],
                  l=ident[:, :], rh=mask[:, 0:2 * P]:
                  E.matmul(o, l, rh, start=True, stop=False))
            for j, kt in ((0, qt - 1), (1, qt + 1), (2, qt)):
                for hp in range(2):
                    col0 = 512 * hp
                    poff = 64 * hp
                    r = p
                    evs[hp] = O("tensor", lambda E,
                                o=bankT[:, col0 + P * j:col0 + P * (j + 1)],
                                l=q_sb[i2][poff:poff + HD, P * r:P * (r + 1)],
                                rh=k_sb[kt % 4][poff:poff + HD, P * r:P * (r + 1)],
                                sp=(j == 2):
                                E.matmul(o, l, rh, start=False, stop=sp),
                                "s_pe" if j == 2 else None)
            return pr.later(evs[0], evs[1])[0]

        def CH(qt, p, evsc_mm):
            bankT = P12 if p % 2 == 0 else P67
            evex = {}
            for hp in range(2):
                h = 2 * p + hp
                col0 = 512 * hp
                pidx = 2 * (p % 2) + hp
                W("vector", evsc_mm, g["nmax"][p % 2])
                evmx = O("vector", lambda E, o=nmax2[p % 2][:, hp:hp + 1],
                         i=bankT[:, col0:col0 + 384]: E.tensor_reduce(
                    o, i, axis=AX.X, op=ALU.max, negate=True), "s_ve")
                W("scalar", evmx, g["ssum"][h], g["probs_buf", pidx])
                evex[hp] = O("scalar", lambda E, o=probs[pidx][:, :],
                             i=bankT[:, col0:col0 + 384],
                             b=nmax2[p % 2][:, hp:hp + 1],
                             a=ssum[:, h:h + 1]: E.activation(
                    o, i, AF.Exp, bias=b, accum_out=a), "s_ae")
            g["nmax"][p % 2] = evex[1]
            g["sc", (p % 2, 0)] = pr.later(g["sc", (p % 2, 0)], evex[0])
            g["sc", (p % 2, 1)] = pr.later(g["sc", (p % 2, 1)], evex[1])
            return evex

        def TP_t(qt, p, evscale):
            evtt = {}
            W("tensor", evscale[0], g["t3"])
            for hp in range(2):
                if hp == 1:
                    W("tensor", evscale[1])
                pidx = 2 * (p % 2) + hp
                for j in range(3):
                    evtt[hp] = O("tensor", lambda E,
                                 o=T3[:, 384 * hp + P * j:384 * hp + P * (j + 1)],
                                 i=probs[pidx][:, P * j:P * (j + 1)], d=ident[:, :]:
                                 E.transpose(o, i, d), "s_pe" if j == 2 else None)
                g["probs_buf", pidx] = evtt[hp]
            return evtt

        def TP_pv(qt, p, evtt, pv_started):
            ring = 0
            if p % 2 == 0:
                W("scalar", evtt[0], evtt[1], g["probsT"][0], g["probsT"][1])
                evte = O("scalar", lambda E, o=probsT[:, ring:ring + 768],
                         i=T3[:, 0:768]: E.activation(o, i, AF.Copy), "s_ae")
            else:
                W("vector", evtt[0], evtt[1], g["probsT"][0], g["probsT"][1])
                evte = O("vector", lambda E, o=probsT[:, ring:ring + 768],
                         i=T3[:, 0:768]: E.tensor_copy(o, i), "s_ve")
            evteA = evteB = evte
            g["t3"] = evte
            import os as _os
            if _os.environ.get("K2_NO_PV"):
                g["pm"][2 * (p % 2)] = evte
                g["pm"][2 * (p % 2) + 1] = evte
                g["probsT"][0] = evte
                g["probsT"][1] = evte
                return evte
            W("tensor", evteA, evteB, vevac[qt - 1], vevac[qt], vevac[qt + 1])
            if p == 0:
                W("tensor", g["p45"])
            evpv = None
            for hp in range(2):
                h = 2 * p + hp
                poff = 64 * hp
                for j, kt in ((0, qt - 1), (1, qt + 1), (2, qt)):
                    evpv = O("tensor", lambda E,
                             o=P45[poff:poff + HD, P * p:P * (p + 1)],
                             l=v_sb[kt % 4][:, HD * h:HD * (h + 1)],
                             rh=probsT[:, ring + 384 * hp + P * j:
                                       ring + 384 * hp + P * (j + 1)],
                             st=(j == 0), sp=(j == 2), tp=(0, poff):
                             E.matmul(o, l, rh, start=st, stop=sp,
                                      tile_position=tp),
                             "s_pe" if (j == 2 and hp == 1) else None)
            g["probsT"][0] = evpv
            g["probsT"][1] = evpv
            return evpv

        def stage_B(qt, interleave_t=None, pre=None):
            import os as _os
            NPAIRS = int(_os.environ.get("K2_PAIRS", 8))
            NO_TP = bool(_os.environ.get("K2_NO_TP"))
            NO_TAIL = bool(_os.environ.get("K2_NO_TAIL"))
            i2, i3 = qt % 2, qt % 3
            mask = m_q2 if qt == 2 else (m_q17 if qt == 17 else m_mid)
            pv_started = set()
            evpv_by_p = {}
            sc_ev = {}
            evscale_by_p = {}
            if pre:
                sc_ev[0] = pre["sc0"]
                evscale_by_p[0] = pre["ch0"]
            else:
                sc_ev[0] = SC(qt, 0, mask)
            sc_ev[1] = SC(qt, 1, mask)
            if 0 not in evscale_by_p:
                evscale_by_p[0] = CH(qt, 0, sc_ev[0])
            if 1 not in evscale_by_p:
                evscale_by_p[1] = CH(qt, 1, sc_ev[1])
            for p in range(NPAIRS):
                evtt = TP_t(qt, p, evscale_by_p[p])
                if p + 2 <= NPAIRS - 1:
                    sc_ev[p + 2] = SC(qt, p + 2, mask)
                    evscale_by_p[p + 2] = CH(qt, p + 2, sc_ev[p + 2])
                if p == 0:
                    stage_C1b()
                if p == 1 and interleave_t is not None:
                    A1_T(interleave_t, a1_elem_ev[interleave_t])
                if NO_TP:
                    continue
                evpv_by_p[p] = TP_pv(qt, p, evtt, pv_started)
                if p == 5:
                    W("vector", ch_ev_sums(evscale_by_p, 5), g["rsumg"])
                    O("vector", lambda E, o=rsum[:, 0:12], i=ssum[:, 0:12]:
                      E.reciprocal(o, i), "s_ve")
                    evrc1 = ("s_ve", pr.cnt["s_ve"], "vector")
                    pr.wait("vector", (evrc1[0], evrc1[1], "_bar"))
                    W("vector", g["delta"])
                    rb1 = bass.AP(rsum, 0, [[NH, P], [2, 6], [1, 2], [0, 64]])
                    evrcb1 = O("vector", lambda E, o=delta[:, 0:768], i=rb1:
                               E.tensor_copy(o, i), "s_ve")
                    for h in range(12):
                        g["ssum"][h] = evrcb1
                    W("tensor", evrcb1, g["t3"])
                    evsm1 = None
                    for pp in range(6):
                        evsm1 = O("tensor", lambda E,
                                  o=T3[:, P * pp:P * (pp + 1)],
                                  i=delta[:, P * pp:P * (pp + 1)],
                                  d=ident[:, :]: E.transpose(o, i, d),
                                  "s_pe" if pp == 5 else None)
                    W("vector", evsm1)
                    evse1 = O("vector", lambda E, o=s_sb[:, 0:768],
                              i=T3[:, 0:768]: E.tensor_copy(o, i), "s_ve")
                    g["t3"] = evse1
                    g["rsumg"] = evse1
                    g["delta"] = pr.later(g["delta"], evsm1)
                    oA = outfm[:, 0:512].rearrange("a (b k) -> a b k", k=P)
                    pA = P45[:, 0:512].rearrange("a (b k) -> a b k", k=P)
                    sA = s_sb[:, 0:512].rearrange("a (b k) -> a b k", k=P)
                    W("vector", evpv_by_p[3], evse1, g["outfm"])
                    evoeA = O("vector", lambda E, o=oA, a=pA, b=sA:
                              E.tensor_tensor(o, a, b, op=ALU.mult), "s_ve")
                if p == 6 and interleave_t is not None:
                    g["p12"] = pr.later(g["p12"], g["sc", (0, 0)],
                                        g["sc", (0, 1)])
                    mm_v(interleave_t)
            if NO_TP or NPAIRS < 8 or NO_TAIL:
                scores_done[qt] = sc_ev[NPAIRS - 1]
                pv_done[qt] = evpv_by_p.get(NPAIRS - 1, sc_ev[NPAIRS - 1])
                g["k"][(qt - 1) % 4] = scores_done[qt]
                g["v"][(qt - 1) % 4] = pv_done[qt]
                g["q"][i2] = scores_done[qt]
                return None
            scores_done[qt] = sc_ev[7]
            pv_done[qt] = evpv_by_p[7]
            W("vector", g["sc", (1, 1)], g["rsumg"])
            O("vector", lambda E, o=rsum[:, 12:16], i=ssum[:, 12:16]:
              E.reciprocal(o, i), "s_ve")
            evrc = ("s_ve", pr.cnt["s_ve"], "vector")
            pr.wait("vector", (evrc[0], evrc[1], "_bar"))
            rbview = bass.AP(rsum, 12, [[NH, P], [2, 2], [1, 2], [0, 64]])
            evrcb = O("vector", lambda E, o=delta[:, 768:1024], i=rbview:
                      E.tensor_copy(o, i), "s_ve")
            for h in range(12, NH):
                g["ssum"][h] = evrcb
            W("tensor", evrcb, g["t3"])
            evsm = None
            for pp in range(6, NCH):
                evsm = O("tensor", lambda E, o=T3[:, P * pp:P * (pp + 1)],
                         i=delta[:, P * pp:P * (pp + 1)], d=ident[:, :]:
                         E.transpose(o, i, d), "s_pe" if pp == NCH - 1 else None)
            W("vector", evsm)
            evse = O("vector", lambda E, o=s_sb[:, 768:1024], i=T3[:, 768:1024]:
                     E.tensor_copy(o, i), "s_ve")
            g["t3"] = evse
            g["rsumg"] = evse
            g["delta"] = pr.later(g["delta"], evsm)
            oB = outfm[:, 512:1024].rearrange("a (b k) -> a b k", k=P)
            pB = P45[:, 512:1024].rearrange("a (b k) -> a b k", k=P)
            sB = s_sb[:, 512:1024].rearrange("a (b k) -> a b k", k=P)
            W("vector", evpv_by_p[7], evse, g["outfm"])
            evoeB = O("vector", lambda E, o=oB, a=pB, b=sB:
                      E.tensor_tensor(o, a, b, op=ALU.mult), "s_ve")
            g["p45"] = evoeB
            g["k"][(qt - 1) % 4] = scores_done[qt]
            g["v"][(qt - 1) % 4] = pv_done[qt]
            g["q"][i2] = scores_done[qt]

            W("tensor", evoeA, wdef["wo"], g["p67"],
              g["sc", (1, 0)], g["sc", (1, 1)])
            evap = None
            for c in range(NCH):
                if c == 4:
                    W("tensor", evoeB)
                for b in range(2):
                    evap = O("tensor", lambda E,
                             o=P67[:, 512 * b:512 * (b + 1)],
                             l=outfm[:, P * c:P * (c + 1)],
                             rh=wout[:, D * c + 512 * b:D * c + 512 * (b + 1)],
                             st=(c == 0), sp=(c == NCH - 1):
                             E.matmul(o, l, rh, start=st, stop=sp),
                             "s_pe" if (b == 1 and c == NCH - 1) else None)
            g["outfm"] = evap

            W("vector", evap, g["a2buf"], g["stat2"][i2])
            eva2 = O("vector", lambda E, o=a2buf[:, :], i=x_in[i3][:, :],
                     s=C(TWO), pp=P67[:, :], a=s2s[i2][:, :]:
                     E.scalar_tensor_tensor(o, i, s, pp, op0=ALU.mult,
                                            op1=ALU.add, accum_out=a), "s_ve")
            a2ev[qt] = eva2
            g["p67"] = eva2
            g["x_in"][i3] = pr.later(g["x_in"][i3], eva2)

            evsq, evr = stats(a2buf[:, :], s2s[i2], e22[i2], m2[i2], p2s[i2],
                              r2[i2], eva2, extra=g["stat2"][i2])
            W("scalar", evr)
            evmr = O("scalar", lambda E, o=lnvs[:, :], i=m2[i2][:, :],
                     s=r2[i2][:, :]: E.activation(o, i, AF.Copy, scale=s),
                     "s_ae")
            W("vector", evr, g["y_tm"][i2])
            evy = O("vector", lambda E, o=y_tm[i2][:, :], i=a2buf[:, :],
                    a=m2[i2][:, :], b=r2[i2][:, :]: E.tensor_scalar(
                o, i, a, b, op0=ALU.add, op1=ALU.mult), "s_ve")
            g["stat2"][i2] = evy
            W("scalar", evmr, g["ybf"])
            evyb = O("scalar", lambda E, o=ybf[:, :], i=a2buf[:, :],
                     s=r2[i2][:, :], b=lnvs[:, :]: E.activation(
                o, i, AF.Identity, scale=s, bias=b), "s_ae")
            g["a2buf"] = pr.later(evy, evsq, evyb)
            g["stat2"][i2] = pr.later(evy, evmr, evyb)
            return evyb

        def stage_B_tail(qt, evyb, interleave_t):
            if evyb is None:
                return
            i2 = qt % 2
            if interleave_t is not None:
                mm_q(interleave_t)
            W("tensor", evyb, g["t0"])
            evyt = None
            for c in range(NCH):
                evyt = O("tensor", lambda E, o=T0[:, P * c:P * (c + 1)],
                         i=ybf[:, P * c:P * (c + 1)], d=ident[:, :]:
                         E.transpose(o, i, d), "s_pe" if c == NCH - 1 else None)
            g["ybf"] = evyt
            W("scalar", evyt, g["y_fm"][i2])
            yview = y_fm[i2][:, :].rearrange("a (c k) -> a c k", k=130)[:, :, 2:130]
            t0view = T0[:, :].rearrange("a (c k) -> a c k", k=P)
            evye = O("scalar", lambda E, o=yview, i=t0view:
                     E.activation(o, i, AF.Copy), "s_ae")
            yevac[qt] = evye
            g["t0"] = evye
            if interleave_t is not None:
                mm_k(interleave_t)

        c2_pend = []
        c1b_pend = []

        def stage_C1b():
            if not c1b_pend:
                return
            ct, evr_, evks = c1b_pend.pop(0)
            i2 = ct % 2
            W("scalar", evr_, g["sig"])
            eve1 = O("scalar", lambda E, o=sig[:, :], i=P45[:, :], s=C(NEGONE):
                     E.activation(o, i, AF.Exp, scale=s), "s_ae")
            W("scalar", eve1, g["ae_scr"])
            evsp = O("scalar", lambda E, o=ae_scr[:, :], i=sig[:, :], b=C(ONE):
                     E.activation(o, i, AF.Ln, bias=b), "s_ae")
            W("scalar", evsp)
            evsg = O("scalar", lambda E, o=sig[:, :], i=ae_scr[:, :],
                     s=C(NEGONE): E.activation(o, i, AF.Exp, scale=s), "s_ae")
            g["ae_scr"] = evsg
            W("tensor", evks, wdef["wv"], eve1, g["p45"])
            evvo = None
            for b in range(2):
                for c in range(NCH):
                    evvo = O("tensor", lambda E,
                             o=P45[:, 512 * b:512 * (b + 1)],
                             l=kact[:, P * c:P * (c + 1)],
                             rh=wv[:, D * c + 512 * b:D * c + 512 * (b + 1)],
                             st=(c == 0), sp=(c == NCH - 1):
                             E.matmul(o, l, rh, start=st, stop=sp),
                             "s_pe" if (b == 1 and c == NCH - 1) else None)
            g["kact"] = evvo
            W("vector", evvo, evsg)
            O("vector", lambda E, o=sig[:, :], i=P45[:, :], s=C(ZERO):
              E.scalar_tensor_tensor(o, i, s, o, op0=ALU.add, op1=ALU.mult),
              "s_ve")
            evt1 = ("s_ve", pr.cnt["s_ve"], "vector")
            g["p45"] = evt1
            g["sig"] = evt1
            c2_pend.append((ct, evt1))

        def stage_C2():
            if not c2_pend:
                return
            ct, evt1 = c2_pend.pop(0)
            i2 = ct % 2
            W("vector", evt1, g["stat3"][i2])
            eva3 = O("vector", lambda E, o=sig[:, :], i=sig[:, :], s=C(ZERO),
                     t=y_tm[i2][:, :], a=s3s[i2][:, :]:
                     E.scalar_tensor_tensor(o, i, s, t, op0=ALU.add,
                                            op1=ALU.add, accum_out=a), "s_ve")
            g["sig"] = eva3
            g["y_tm"][i2] = pr.later(g["y_tm"][i2], eva3)
            evsq, evr = stats(sig[:, :], s3s[i2], e23[i2], m3[i2], p3s[i2],
                              r3[i2], eva3, extra=g["stat3"][i2])
            W("vector", evr, g["out_sb"][i2])
            evout = O("vector", lambda E, o=out_sb[i2][:, :], i=sig[:, :],
                      a=m3[i2][:, :], b=r3[i2][:, :]: E.tensor_scalar(
                o, i, a, b, op0=ALU.add, op1=ALU.mult), "s_ve")
            g["stat3"][i2] = evout
            g["sig"] = pr.later(g["sig"], evout, evsq)
            W("gpsimd", evout)
            evst = dma("gpsimd", f"s_st{i2}",
                       out_d[P * (ct - CT0):P * (ct - CT0 + 1), :],
                       out_sb[i2][:, :])
            g["out_sb"][i2] = evst

        def stage_C(ct):
            i2, pp2 = ct % 2, (ct - 1) % 2

            W("vector", yevac[ct], yevac[ct - 1], g["y_fm"][i2], ev_cv, ev_c)
            evpc = None
            for c in range(NCH):
                if ct == CT0:
                    evpc = O("vector", lambda E,
                             o=y_fm[i2][:, 130 * c + 1:130 * c + 2],
                             i=y_fm[pp2][:, 130 * c + 129:130 * c + 130],
                             s=gate[:, :]: E.tensor_scalar(
                        o, i, s, None, op0=ALU.mult),
                        "s_ve" if c == NCH - 1 else None)
                else:
                    evpc = O("vector", lambda E,
                             o=y_fm[i2][:, 130 * c + 1:130 * c + 2],
                             i=y_fm[pp2][:, 130 * c + 129:130 * c + 130]:
                             E.tensor_copy(o, i), "s_ve" if c == NCH - 1 else None)
            g["y_fm"][pp2] = pr.later(g["y_fm"][pp2], evpc)

            own = y_fm[i2][:, :].rearrange("a (c k) -> a c k", k=130)[:, :, 2:130]
            prev = y_fm[i2][:, :].rearrange("a (c k) -> a c k", k=130)[:, :, 1:129]
            dview = delta[:, :].rearrange("a (c k) -> a c k", k=P)

            W("vector", evpc, g["delta"])
            evd = O("vector", lambda E, o=dview, a=own, b=prev:
                    E.tensor_tensor(o, a, b, op=ALU.subtract), "s_ve")
            W("vector", evd, g["xr"])
            evxr = None
            for c in range(NCH):
                evxr = O("vector", lambda E, o=xr_fm[:, P * c:P * (c + 1)],
                         i=delta[:, P * c:P * (c + 1)], s=mur[:, c:c + 1],
                         t=y_fm[i2][:, 130 * c + 1:130 * c + 129]:
                         E.scalar_tensor_tensor(o, i, s, t, op0=ALU.mult,
                                                op1=ALU.add),
                         "s_ve" if c == NCH - 1 else None)
            W("vector", evd, g["xk"])
            evxk = None
            for c in range(NCH):
                evxk = O("vector", lambda E, o=xk_fm[:, P * c:P * (c + 1)],
                         i=delta[:, P * c:P * (c + 1)], s=muk[:, c:c + 1],
                         t=y_fm[i2][:, 130 * c + 1:130 * c + 129]:
                         E.scalar_tensor_tensor(o, i, s, t, op0=ALU.mult,
                                                op1=ALU.add),
                         "s_ve" if c == NCH - 1 else None)
            g["delta"] = pr.later(evxr, evxk)
            g["y_fm"][i2] = pr.later(g["y_fm"][i2], evxr, evxk)

            W("tensor", evxr, wdef["wr"], g["p45"])
            evr_ = None
            for b in range(2):
                for c in range(NCH):
                    evr_ = O("tensor", lambda E,
                             o=P45[:, 512 * b:512 * (b + 1)],
                             l=xr_fm[:, P * c:P * (c + 1)],
                             rh=wr[:, D * c + 512 * b:D * c + 512 * (b + 1)],
                             st=(c == 0), sp=(c == NCH - 1):
                             E.matmul(o, l, rh, start=st, stop=sp),
                             "s_pe" if (b == 1 and c == NCH - 1) else None)
            g["xr"] = evr_

            W("tensor", evxk, wdef["wk"], g["p67"])
            evkf = None
            for r_ in range(NCH):
                for c in range(NCH):
                    evkf = O("tensor", lambda E,
                             o=P67[:, P * r_:P * (r_ + 1)],
                             l=wk[:, D * c + P * r_:D * c + P * (r_ + 1)],
                             rh=xk_fm[:, P * c:P * (c + 1)],
                             st=(r_ % 4 == 0 and c == 0),
                             sp=(r_ % 4 == 3 and c == NCH - 1):
                             E.matmul(o, l, rh, start=st, stop=sp),
                             "s_pe" if (r_ == NCH - 1 and c == NCH - 1) else None)
            g["xk"] = evkf

            W("scalar", evkf, g["krelu"], g["delta"])
            evkr = O("scalar", lambda E, o=krelu[:, :], i=P67[:, :]:
                     E.activation(o, i, AF.Relu), "s_ae")
            g["p67"] = evkr
            g["sc", (1, 0)] = evkr
            g["sc", (1, 1)] = evkr
            W("vector", evkr, g["kact"])
            evks = O("vector", lambda E, o=kact[:, :], i=krelu[:, :]:
                     E.tensor_mul(o, i, i), "s_ve")
            g["krelu"] = evks
            g["delta"] = pr.later(g["delta"], evks)

            c1b_pend.append((ct, evr_, evks))

        a1_elem_ev = {}
        for t in range(3):
            a1_elem_ev[t] = A1_elem(t)
            A1_T(t, a1_elem_ev[t])
            if QT0 <= t <= QT1:
                mm_q(t)
            mm_k(t)
            mm_v(t)
        for c in range(NCH):
            wdef["wo"] = dma("gpsimd", "s_wo", wout[:, D * c:D * (c + 1)],
                             wout_d[P * c:P * (c + 1), :])
        for c in range(NCH):
            wdef["wr"] = dma("gpsimd", "s_wr", wr[:, D * c:D * (c + 1)],
                             wr_d[P * c:P * (c + 1), :])
        for c in range(NCH):
            wdef["wk"] = dma("gpsimd", "s_wk", wk[:, D * c:D * (c + 1)],
                             wk_d[P * c:P * (c + 1), :])
        for c in range(NCH):
            wdef["wv"] = dma("gpsimd", "s_wv", wv[:, D * c:D * (c + 1)],
                             wv_d[P * c:P * (c + 1), :])

        import os as _os
        ILV = not _os.environ.get("K2_NO_ILV")
        MAXT = int(_os.environ.get("K2_MAXT", NT - 1))
        pre_b = None
        for t in range(1, MAXT):
            nxt = t + 2 if t + 2 <= NT - 1 else None
            if nxt is not None:
                W("sync", g["x_in"][nxt % 3])
                ld_ev[nxt] = dma("sync", f"s_x{nxt % 3}", x_in[nxt % 3][:, :],
                                 x_d[P * nxt:P * (nxt + 1), :])
                a1_elem_ev[nxt] = A1_elem(nxt)
            if not ILV and nxt is not None:
                A1_T(nxt, a1_elem_ev[nxt])
                mm_v(nxt)
                mm_q(nxt)
                mm_k(nxt)
            evyb = stage_B(t, interleave_t=(nxt if ILV else None), pre=pre_b)
            stage_B_tail(t, evyb, (nxt if ILV else None))
            pre_b = None
            if t + 1 < MAXT:
                mask_n = m_q2 if t + 1 == 2 else (m_q17 if t + 1 == 17 else m_mid)
                sc0 = SC(t + 1, 0, mask_n)
                ch0 = CH(t + 1, 0, sc0)
                pre_b = {"sc0": sc0, "ch0": ch0}
            if t >= CT0:
                stage_C(t)
            stage_C2()

        stage_C1b()
        stage_C2()
        stage_C2()
        if pr.cnt.get("s_st0"):
            W("gpsimd", ("s_st0", pr.cnt["s_st0"], "_dma"))
        if pr.cnt.get("s_st1"):
            W("gpsimd", ("s_st1", pr.cnt["s_st1"], "_dma"))

        with nc.Block() as block:
            def make_runner(name):
                def run(E):
                    for item in pr.ops[name]:
                        if item[0] == "w":
                            E.wait_ge(sems[item[1]], item[2])
                        else:
                            ins = item[1](E)
                            if item[2] is not None:
                                ins.then_inc(sems[item[2]], item[3])
                return run

            block.sync(make_runner("sync"))
            block.vector(make_runner("vector"))
            block.scalar(make_runner("scalar"))
            block.tensor(make_runner("tensor"))
            block.gpsimd(make_runner("gpsimd"))

    return nc


_CACHE = {}


def _get_nc():
    if "nc" not in _CACHE:
        _CACHE["nc"] = build_program()
    return _CACHE["nc"]


def _build_masks(h):
    i = np.arange(P)[:, None]
    j = np.arange(3 * P)[None, :]
    band = (j - i >= 0) & (j - i <= 256)
    NEG = -1e9
    mid = np.where(band, 0.0, NEG).astype(np.float32)
    q2 = np.where(band & (j >= 128), 0.0, NEG).astype(np.float32) if h == 0 else mid
    q17 = np.where(band & (j < 256), 0.0, NEG).astype(np.float32) if h == 1 else mid
    perm = np.r_[0:128, 256:384, 128:256]
    bf = ml_dtypes.bfloat16
    return (mid[:, perm].astype(bf), q2[:, perm].astype(bf),
            q17[:, perm].astype(bf))


def kernel(L0_attn_x, L0_attn_w_qkv, L0_attn_w_out, L0_attn_gamma, L0_attn_beta,
           L0_ffn_W_r, L0_ffn_W_k, L0_ffn_W_v, L0_ffn_mu_r, L0_ffn_mu_k):
    x = np.asarray(L0_attn_x, dtype=np.float32)
    wqkvT = np.ascontiguousarray(np.asarray(L0_attn_w_qkv, dtype=np.float32).T)
    woutT = np.ascontiguousarray(np.asarray(L0_attn_w_out, dtype=np.float32).T)
    wrT = np.ascontiguousarray(np.asarray(L0_ffn_W_r, dtype=np.float32).T)
    wkT = np.ascontiguousarray(np.asarray(L0_ffn_W_k, dtype=np.float32).T)
    wvT = np.ascontiguousarray(np.asarray(L0_ffn_W_v, dtype=np.float32).T)

    def fm(vec):
        return np.ascontiguousarray(
            np.asarray(vec, dtype=np.float32).reshape(NCH, P).T)

    gamma_fm, beta_fm = fm(L0_attn_gamma), fm(L0_attn_beta)
    mur_fm, muk_fm = fm(L0_ffn_mu_r), fm(L0_ffn_mu_k)
    ident = np.eye(P, dtype=np.float32).astype(ml_dtypes.bfloat16)

    in_maps = []
    for core in range(8):
        b, h = core // 2, core % 2
        s = h * 2048
        xp = np.zeros((NT * P, D), np.float32)
        lo, hi = s - 256, s + 2176
        clo, chi = max(lo, 0), min(hi, 4096)
        xp[clo - lo:chi - lo] = x[b, clo:chi]
        mid, q2, q17 = _build_masks(h)
        in_maps.append({
            "x": xp, "wqkvT": wqkvT, "woutT": woutT, "wrT": wrT, "wkT": wkT,
            "wvT": wvT, "gamma_fm": gamma_fm, "beta_fm": beta_fm,
            "mur_fm": mur_fm, "muk_fm": muk_fm,
            "gate": np.full((P, 1), 0.0 if h == 0 else 1.0, np.float32),
            "ident": ident, "mask_mid": mid, "mask_q2": q2, "mask_q17": q17,
        })

    nc = _get_nc()
    res = run_bass_kernel_spmd(nc, in_maps, core_ids=list(range(8)))
    out = np.empty((4, 4096, D), np.float32)
    for core in range(8):
        b, h = core // 2, core % 2
        out[b, h * 2048:(h + 1) * 2048] = res.results[core]["out"]
    return out


# version 26
# speedup vs baseline: 1.0202x; 1.0202x over previous
from contextlib import ExitStack
import os
import numpy as np
import ml_dtypes
import concourse.bass as bass
import concourse.mybir as mybir
from concourse.bass_utils import run_bass_kernel_spmd

F32 = mybir.dt.float32
BF16 = mybir.dt.bfloat16
AF = mybir.ActivationFunctionType
ALU = mybir.AluOpType
AX = mybir.AxisListType

P = 128
D = 1024
NCH = 8
NT = 19
QT0, QT1 = 1, 17
CT0, CT1 = 2, 17
NH = 16
HD = 64

ENGS = ("sync", "vector", "scalar", "tensor", "gpsimd")


class Prog:
    def __init__(self, sems):
        self.sems = sems
        self.ops = {e: [] for e in ENGS}
        self.cnt = {}
        self.waited = {e: {} for e in ENGS}

    def wait(self, eng, ev):
        if ev is None:
            return
        if isinstance(ev, list):
            for x in ev:
                self.wait(eng, x)
            return
        sem, val, src = ev
        if self.waited[eng].get(sem, -1) >= val:
            return
        self.waited[eng][sem] = val
        self.ops[eng].append(("w", sem, val))

    def do(self, eng, fn, inc=None, n=1):
        if inc is None:
            self.ops[eng].append(("o", fn, None, 0))
            return None
        self.cnt[inc] = self.cnt.get(inc, 0) + n
        self.ops[eng].append(("o", fn, inc, n))
        return (inc, self.cnt[inc], eng)

    def later(self, *evs):
        best = {}
        for ev in evs:
            if ev is None:
                continue
            if isinstance(ev, list):
                for x in ev:
                    if x is not None and (x[0] not in best or best[x[0]][1] < x[1]):
                        best[x[0]] = x
                continue
            if ev[0] not in best or best[ev[0]][1] < ev[1]:
                best[ev[0]] = ev
        return list(best.values())


def build_program():
    nc = bass.Bass("TRN2", target_bir_lowering=False)

    x_d = nc.dram_tensor("x", [NT * P, D], F32, kind="ExternalInput")
    wqkv_d = nc.dram_tensor("wqkvT", [D, 3 * D], F32, kind="ExternalInput")
    wout_d = nc.dram_tensor("woutT", [D, D], F32, kind="ExternalInput")
    wr_d = nc.dram_tensor("wrT", [D, D], F32, kind="ExternalInput")
    wk_d = nc.dram_tensor("wkT", [D, D], F32, kind="ExternalInput")
    wv_d = nc.dram_tensor("wvT", [D, D], F32, kind="ExternalInput")
    gam_d = nc.dram_tensor("gamma_fm", [P, NCH], F32, kind="ExternalInput")
    bet_d = nc.dram_tensor("beta_fm", [P, NCH], F32, kind="ExternalInput")
    mur_d = nc.dram_tensor("mur_fm", [P, NCH], F32, kind="ExternalInput")
    muk_d = nc.dram_tensor("muk_fm", [P, NCH], F32, kind="ExternalInput")
    gate_d = nc.dram_tensor("gate", [P, 1], F32, kind="ExternalInput")
    id_d = nc.dram_tensor("ident", [P, P], BF16, kind="ExternalInput")
    mmid_d = nc.dram_tensor("mask_mid", [P, 384], BF16, kind="ExternalInput")
    mq2_d = nc.dram_tensor("mask_q2", [P, 384], BF16, kind="ExternalInput")
    mq17_d = nc.dram_tensor("mask_q17", [P, 384], BF16, kind="ExternalInput")
    out_d = nc.dram_tensor("out", [16 * P, D], F32, kind="ExternalOutput")

    with ExitStack() as ctx:
        e = ctx.enter_context

        wqkv = e(nc.sbuf_tensor("wqkv", [P, NCH * 3 * D], BF16))
        wout = e(nc.sbuf_tensor("wout", [P, NCH * D], BF16))
        wr = e(nc.sbuf_tensor("wr", [P, NCH * D], BF16))
        wk = e(nc.sbuf_tensor("wk", [P, NCH * D], BF16))
        wv = e(nc.sbuf_tensor("wv", [P, NCH * D], BF16))
        gam = e(nc.sbuf_tensor("gam", [P, NCH], F32))
        bet = e(nc.sbuf_tensor("bet", [P, NCH], F32))
        mur = e(nc.sbuf_tensor("mur", [P, NCH], F32))
        muk = e(nc.sbuf_tensor("muk", [P, NCH], F32))
        gate = e(nc.sbuf_tensor("gate_t", [P, 1], F32))
        ident = e(nc.sbuf_tensor("ident_t", [P, P], BF16))
        m_mid = e(nc.sbuf_tensor("m_mid", [P, 384], BF16))
        m_q2 = e(nc.sbuf_tensor("m_q2", [P, 384], BF16))
        m_q17 = e(nc.sbuf_tensor("m_q17", [P, 384], BF16))
        s_sb = e(nc.sbuf_tensor("s_sb", [P, D], BF16))
        consts = e(nc.sbuf_tensor("consts", [P, 10], F32))
        EPS, NHALF, INV_D, EIGHTH, ZERO, TWO, ONE, NEGONE, INV_SQD, NINV_D = range(10)

        x_in = [e(nc.sbuf_tensor(f"x_in{i}", [P, D], F32)) for i in range(3)]
        xnz = [e(nc.sbuf_tensor(f"xnz{i}", [P, D], BF16)) for i in range(2)]
        xn_fm = [e(nc.sbuf_tensor(f"xn_fm{i}", [P, D], BF16)) for i in range(2)]
        q_sb = [e(nc.sbuf_tensor(f"q_sb{i}", [P, D], BF16)) for i in range(2)]
        k_sb = [e(nc.sbuf_tensor(f"k_sb{i}", [P, D], BF16)) for i in range(4)]
        v_sb = [e(nc.sbuf_tensor(f"v_sb{i}", [P, D], BF16)) for i in range(4)]
        probs = [e(nc.sbuf_tensor(f"probs{i}", [P, 384], BF16)) for i in range(4)]
        probsT = e(nc.sbuf_tensor("probsT", [P, 768], BF16))
        outfm = e(nc.sbuf_tensor("outfm", [P, D], BF16))
        a2buf = e(nc.sbuf_tensor("a2buf", [P, D], F32))
        y_tm = [e(nc.sbuf_tensor(f"y_tm{i}", [P, D], F32)) for i in range(2)]
        ybf = e(nc.sbuf_tensor("ybf", [P, D], BF16))
        y_fm = [e(nc.sbuf_tensor(f"y_fm{i}", [P, NCH * 130], BF16)) for i in range(2)]
        delta = e(nc.sbuf_tensor("delta", [P, D], BF16))
        xr_fm = e(nc.sbuf_tensor("xr_fm", [P, D], BF16))
        xk_fm = e(nc.sbuf_tensor("xk_fm", [P, D], BF16))
        kact = e(nc.sbuf_tensor("kact", [P, D], BF16))
        krelu = delta
        sig = e(nc.sbuf_tensor("sig", [P, D], F32))
        ae_scr = e(nc.sbuf_tensor("ae_scr", [P, D], F32))
        out_sb = [e(nc.sbuf_tensor(f"out_sb{i}", [P, D], F32)) for i in range(2)]

        def slots(name, n=2):
            return [e(nc.sbuf_tensor(f"{name}{i}", [P, 1], F32)) for i in range(n)]
        sx1, e21, m1, p1s, r1 = slots("sx1"), slots("e21"), slots("m1"), slots("p1sl"), slots("r1")
        s2s, e22, m2, p2s, r2 = slots("s2s"), slots("e22"), slots("m2"), slots("p2s"), slots("r2")
        s3s, e23, m3, p3s, r3 = slots("s3s"), slots("e23"), slots("m3"), slots("p3s"), slots("r3")
        nm2s = slots("nm2s", 1)[0]
        lnvs = slots("lnvs", 1)[0]
        nmax2 = [e(nc.sbuf_tensor(f"nmax2_{i}", [P, 2], F32)) for i in range(2)]
        ssum = e(nc.sbuf_tensor("ssum", [P, NH], F32))
        rsum = e(nc.sbuf_tensor("rsum", [P, NH], F32))

        T0 = e(nc.psum_tensor("T0", [P, 1024], BF16))
        T3 = e(nc.psum_tensor("T3", [P, 1024], BF16))
        P12 = e(nc.psum_tensor("P12", [P, 1024], F32))
        P45 = e(nc.psum_tensor("P45", [P, 1024], F32))
        P67 = e(nc.psum_tensor("P67", [P, 1024], F32))

        sems = {}
        for s in ("s_c", "s_x0", "s_x1", "s_x2", "s_wq", "s_wo", "s_wr",
                  "s_wk", "s_wv", "s_st0", "s_st1",
                  "s_pe", "s_ve", "s_ae", "s_gp",
                  *[f"s_wq{i}" for i in range(NCH)]):
            sems[s] = e(nc.semaphore(s))

        pr = Prog(sems)

        def W(eng, *evs):
            for ev in evs:
                pr.wait(eng, ev)

        O = pr.do

        def C(j):
            return consts[:, j:j + 1]

        def dma(eng, sem, out_ap, in_ap):
            cnt = pr.cnt.get(sem, 0) + 16
            pr.cnt[sem] = cnt
            pr.ops[eng].append(
                ("o", lambda E, o=out_ap, i=in_ap: E.dma_start(o, i), sem, 16))
            return (sem, cnt, "_dma")

        ld_ev = {}
        ev = None
        for dst, src in ((gam, gam_d), (bet, bet_d), (mur, mur_d), (muk, muk_d),
                         (gate, gate_d), (ident, id_d), (m_mid, mmid_d),
                         (m_q2, mq2_d), (m_q17, mq17_d)):
            ev = dma("sync", "s_c", dst[:, :], src[:, :])
        ev_c = ev

        evm = None
        for j, val in ((EPS, 1e-5), (NHALF, -0.5), (INV_D, 1.0 / D),
                       (EIGHTH, 0.125), (ZERO, 0.0), (TWO, 2.0), (ONE, 1.0),
                       (NEGONE, -1.0), (INV_SQD, 1.0 / 32.0),
                       (NINV_D, -1.0 / D)):
            evm = O("vector", lambda E, jj=j, vv=val: E.memset(
                consts[:, jj:jj + 1], vv), "s_ve")
        ev_cv = evm

        ev_wq_c = []
        for c in range(NCH):
            ev_wq_c.append(dma("gpsimd", f"s_wq{c}",
                               wqkv[:, 3 * D * c:3 * D * (c + 1)],
                               wqkv_d[P * c:P * (c + 1), :]))
        ev_wq = ev_wq_c[-1]
        wdef = {}

        for t in range(3):
            ld_ev[t] = dma("sync", f"s_x{t % 3}", x_in[t % 3][:, :],
                           x_d[P * t:P * (t + 1), :])

        g = {
            "t0": None, "t3": None,
            "p12": None, "p45": None, "p67": None,
            "xn_fm": [None, None], "q": [None, None],
            "k": [None] * 4, "v": [None] * 4,
            "x_in": [None] * 3, "xnz": [None, None],
            "pm": [None] * 4, "probsT": [None, None],
            "a2buf": None, "y_tm": [None, None], "y_fm": [None, None],
            "out_sb": [None, None],
            "stat1": [None, None], "stat2": [None, None], "stat3": [None, None],
            "nmax": [None, None], "ssum": [None] * NH, "rsumg": None,
            "ybf": None, "outfm": None, "delta": None, "ae_scr": None,
            ("probs_buf", 0): None, ("probs_buf", 1): None,
            ("probs_buf", 2): None, ("probs_buf", 3): None,
            "xr": None, "xk": None, "krelu": None, "kact": None, "sig": None,
        }
        for par in (0, 1):
            for hp in (0, 1):
                g["sc", (par, hp)] = None
        kevac, vevac, qevac = {}, {}, {}
        scores_done, pv_done, yevac, a2ev = {}, {}, {}, {}
        xnev = {}

        def stats(src_ap, sumslot, e2slot, mslot, epslot, rslot, evsum, extra=None):
            W("scalar", evsum, ev_cv, extra, g["ae_scr"])
            evsq = O("scalar", lambda E, o=ae_scr[:, :], i=src_ap,
                     a=e2slot[:, :], s=C(INV_SQD): E.activation(
                o, i, AF.Square, scale=s, accum_out=a), "s_ae")
            W("scalar", evsq)
            ev1 = O("scalar", lambda E, o=epslot[:, :], i=e2slot[:, :],
                    b=C(EPS): E.activation(o, i, AF.Identity, bias=b), "s_ae")
            ev2 = O("scalar", lambda E, o=mslot[:, :], i=sumslot[:, :],
                    s=C(NINV_D): E.activation(o, i, AF.Copy, scale=s), "s_ae")
            W("scalar", ev2)
            ev3 = O("scalar", lambda E, o=nm2s[:, :], i=mslot[:, :]:
                    E.activation(o, i, AF.Square), "s_ae")
            W("scalar", ev1, ev3)
            ev4 = O("scalar", lambda E, o=lnvs[:, :], i=nm2s[:, :], s=C(NEGONE),
                    b=epslot[:, :]: E.activation(o, i, AF.Ln, scale=s, bias=b),
                    "s_ae")
            W("scalar", ev4)
            evr = O("scalar", lambda E, o=rslot[:, :], i=lnvs[:, :],
                    s=C(NHALF): E.activation(o, i, AF.Exp, scale=s), "s_ae")
            g["ae_scr"] = evsq
            return evsq, evr

        def A1_elem(t):
            i3, i2 = t % 3, t % 2
            xin = x_in[i3]
            W("vector", ld_ev[t], ev_cv, g["stat1"][i2])
            evsum = O("vector", lambda E, o=sx1[i2][:, :], i=xin[:, :]:
                      E.tensor_reduce(o, i, axis=AX.X, op=ALU.add), "s_ve")
            evsq, evr = stats(xin[:, :], sx1[i2], e21[i2], m1[i2], p1s[i2],
                              r1[i2], evsum, extra=pr.later(ld_ev[t], g["stat1"][i2]))
            W("vector", evr, g["xnz"][i2])
            evxnz = O("vector", lambda E, o=xnz[i2][:, :], i=xin[:, :],
                      a=m1[i2][:, :], b=r1[i2][:, :]: E.tensor_scalar(
                o, i, a, b, op0=ALU.add, op1=ALU.mult), "s_ve")
            g["stat1"][i2] = evxnz
            g["x_in"][i3] = pr.later(evxnz, evsq)
            return evxnz

        def A1_T(t, evxnz):
            i2 = t % 2
            W("tensor", evxnz, g["t0"], ev_c)
            evtp = None
            for c in range(NCH):
                evtp = O("tensor", lambda E, o=T0[:, P * c:P * (c + 1)],
                         i=xnz[i2][:, P * c:P * (c + 1)], d=ident[:, :]:
                         E.transpose(o, i, d), "s_pe" if c == NCH - 1 else None)
            g["xnz"][i2] = evtp
            W("scalar", evtp, ev_c, g["xn_fm"][i2])
            evxe = None
            for c in range(NCH):
                evxe = O("scalar", lambda E, o=xn_fm[i2][:, P * c:P * (c + 1)],
                         i=T0[:, P * c:P * (c + 1)], a=gam[:, c:c + 1],
                         b=bet[:, c:c + 1]: E.activation(
                    o, i, AF.Identity, scale=a, bias=b),
                    "s_ae" if c == NCH - 1 else None)
            g["t0"] = evxe
            xnev[t] = evxe
            return evxe

        def mm_q(t):
            i2 = t % 2
            W("tensor", xnev[t], g["p45"], g["q"][i2])
            last = None
            for r in range(NCH):
                for c in range(NCH):
                    W("tensor", ev_wq_c[c])
                    last = O("tensor", lambda E,
                             o=P45[:, P * r:P * (r + 1)],
                             l=wqkv[:, 3 * D * c + P * r:3 * D * c + P * (r + 1)],
                             rh=xn_fm[i2][:, P * c:P * (c + 1)],
                             st=(r % 4 == 0 and c == 0),
                             sp=(r % 4 == 3 and c == NCH - 1):
                             E.matmul(o, l, rh, start=st, stop=sp),
                             "s_pe" if (r == NCH - 1 and c == NCH - 1) else None)
            W("vector", last, g["q"][i2])
            evq = None
            for b in range(2):
                evq = O("vector", lambda E, o=q_sb[i2][:, 512 * b:512 * (b + 1)],
                        i=P45[:, 512 * b:512 * (b + 1)], s=C(EIGHTH):
                        E.tensor_scalar(o, i, s, None, op0=ALU.mult),
                        "s_ve" if b == 1 else None)
            g["p45"] = evq
            qevac[t] = evq
            g["xn_fm_q", t] = last
            return last

        def mm_k(t):
            i2, i4 = t % 2, t % 4
            W("tensor", xnev[t], g["p67"])
            last = None
            for r in range(NCH):
                for c in range(NCH):
                    W("tensor", ev_wq_c[c])
                    last = O("tensor", lambda E,
                             o=P67[:, P * r:P * (r + 1)],
                             l=wqkv[:, 3 * D * c + D + P * r:3 * D * c + D + P * (r + 1)],
                             rh=xn_fm[i2][:, P * c:P * (c + 1)],
                             st=(r % 4 == 0 and c == 0),
                             sp=(r % 4 == 3 and c == NCH - 1):
                             E.matmul(o, l, rh, start=st, stop=sp),
                             "s_pe" if (r == NCH - 1 and c == NCH - 1) else None)
            W("scalar", last, g["k"][i4])
            evke = None
            for b in range(2):
                evke = O("scalar", lambda E, o=k_sb[i4][:, 512 * b:512 * (b + 1)],
                         i=P67[:, 512 * b:512 * (b + 1)]:
                         E.activation(o, i, AF.Copy), "s_ae" if b == 1 else None)
            g["p67"] = evke
            g["sc", (1, 0)] = pr.later(g["sc", (1, 0)], evke)
            g["sc", (1, 1)] = pr.later(g["sc", (1, 1)], evke)
            kevac[t] = evke
            return last

        def mm_v(t):
            i2, i4 = t % 2, t % 4
            W("tensor", xnev[t], g["p12"])
            last = None
            for b in range(2):
                for c in range(NCH):
                    W("tensor", ev_wq_c[c])
                    last = O("tensor", lambda E,
                             o=P12[:, 512 * b:512 * (b + 1)],
                             l=xn_fm[i2][:, P * c:P * (c + 1)],
                             rh=wqkv[:, 3 * D * c + 2 * D + 512 * b:
                                     3 * D * c + 2 * D + 512 * (b + 1)],
                             st=(c == 0), sp=(c == NCH - 1):
                             E.matmul(o, l, rh, start=st, stop=sp),
                             "s_pe" if (b == 1 and c == NCH - 1) else None)
            W("vector", last, g["v"][i4])
            evve = None
            for b in range(2):
                evve = O("vector", lambda E, o=v_sb[i4][:, 512 * b:512 * (b + 1)],
                         i=P12[:, 512 * b:512 * (b + 1)]:
                         E.tensor_copy(o, i), "s_ve" if b == 1 else None)
            g["p12"] = evve
            g["sc", (0, 0)] = pr.later(g["sc", (0, 0)], evve)
            g["sc", (0, 1)] = pr.later(g["sc", (0, 1)], evve)
            vevac[t] = evve
            return last

        def ch_ev_sums(evd, n):
            evs = []
            for pp_ in range(n + 1):
                if pp_ in evd:
                    evs.extend([evd[pp_][0], evd[pp_][1]])
            return pr.later(*evs)

        def SC(qt, p, mask):
            bankT = P12 if p % 2 == 0 else P67
            evs = {}
            W("tensor", qevac[qt], kevac[qt - 1], kevac[qt], kevac[qt + 1],
              ev_c)
            W("tensor", g["sc", (p % 2, 0)], g["sc", (p % 2, 1)])
            i2 = qt % 2
            for hp in range(2):
                col0 = 512 * hp
                O("tensor", lambda E, o=bankT[:, col0:col0 + 2 * P],
                  l=ident[:, :], rh=mask[:, 0:2 * P]:
                  E.matmul(o, l, rh, start=True, stop=False))
            for j, kt in ((0, qt - 1), (1, qt + 1), (2, qt)):
                for hp in range(2):
                    col0 = 512 * hp
                    poff = 64 * hp
                    r = p
                    evs[hp] = O("tensor", lambda E,
                                o=bankT[:, col0 + P * j:col0 + P * (j + 1)],
                                l=q_sb[i2][poff:poff + HD, P * r:P * (r + 1)],
                                rh=k_sb[kt % 4][poff:poff + HD, P * r:P * (r + 1)],
                                sp=(j == 2):
                                E.matmul(o, l, rh, start=False, stop=sp),
                                "s_pe" if j == 2 else None)
            return pr.later(evs[0], evs[1])[0]

        def CH(qt, p, evsc_mm):
            bankT = P12 if p % 2 == 0 else P67
            evex = {}
            for hp in range(2):
                h = 2 * p + hp
                col0 = 512 * hp
                pidx = 2 * (p % 2) + hp
                W("vector", evsc_mm, g["nmax"][p % 2])
                evmx = O("vector", lambda E, o=nmax2[p % 2][:, hp:hp + 1],
                         i=bankT[:, col0:col0 + 384]: E.tensor_reduce(
                    o, i, axis=AX.X, op=ALU.max, negate=True), "s_ve")
                W("scalar", evmx, g["ssum"][h], g["probs_buf", pidx])
                evex[hp] = O("scalar", lambda E, o=probs[pidx][:, :],
                             i=bankT[:, col0:col0 + 384],
                             b=nmax2[p % 2][:, hp:hp + 1],
                             a=ssum[:, h:h + 1]: E.activation(
                    o, i, AF.Exp, bias=b, accum_out=a), "s_ae")
            g["nmax"][p % 2] = evex[1]
            g["sc", (p % 2, 0)] = pr.later(g["sc", (p % 2, 0)], evex[0])
            g["sc", (p % 2, 1)] = pr.later(g["sc", (p % 2, 1)], evex[1])
            return evex

        def TP_t(qt, p, evscale):
            evtt = {}
            W("tensor", evscale[0], g["t3"])
            for hp in range(2):
                if hp == 1:
                    W("tensor", evscale[1])
                pidx = 2 * (p % 2) + hp
                for j in range(3):
                    evtt[hp] = O("tensor", lambda E,
                                 o=T3[:, 384 * hp + P * j:384 * hp + P * (j + 1)],
                                 i=probs[pidx][:, P * j:P * (j + 1)], d=ident[:, :]:
                                 E.transpose(o, i, d), "s_pe" if j == 2 else None)
                g["probs_buf", pidx] = evtt[hp]
            return evtt

        def TP_pv(qt, p, evtt, pv_started):
            ring = 0
            if p % 2 == 0:
                W("scalar", evtt[0], evtt[1], g["probsT"][0], g["probsT"][1])
                evte = O("scalar", lambda E, o=probsT[:, ring:ring + 768],
                         i=T3[:, 0:768]: E.activation(o, i, AF.Copy), "s_ae")
            else:
                W("vector", evtt[0], evtt[1], g["probsT"][0], g["probsT"][1])
                evte = O("vector", lambda E, o=probsT[:, ring:ring + 768],
                         i=T3[:, 0:768]: E.tensor_copy(o, i), "s_ve")
            evteA = evteB = evte
            g["t3"] = evte
            import os as _os
            if _os.environ.get("K2_NO_PV"):
                g["pm"][2 * (p % 2)] = evte
                g["pm"][2 * (p % 2) + 1] = evte
                g["probsT"][0] = evte
                g["probsT"][1] = evte
                return evte
            W("tensor", evteA, evteB, vevac[qt - 1], vevac[qt], vevac[qt + 1])
            if p == 0:
                W("tensor", g["p45"])
            evpv = None
            for hp in range(2):
                h = 2 * p + hp
                poff = 64 * hp
                for j, kt in ((0, qt - 1), (1, qt + 1), (2, qt)):
                    evpv = O("tensor", lambda E,
                             o=P45[poff:poff + HD, P * p:P * (p + 1)],
                             l=v_sb[kt % 4][:, HD * h:HD * (h + 1)],
                             rh=probsT[:, ring + 384 * hp + P * j:
                                       ring + 384 * hp + P * (j + 1)],
                             st=(j == 0), sp=(j == 2), tp=(0, poff):
                             E.matmul(o, l, rh, start=st, stop=sp,
                                      tile_position=tp),
                             "s_pe" if (j == 2 and hp == 1) else None)
            g["probsT"][0] = evpv
            g["probsT"][1] = evpv
            return evpv

        def stage_B(qt, interleave_t=None, pre=None):
            import os as _os
            NPAIRS = int(_os.environ.get("K2_PAIRS", 8))
            NO_TP = bool(_os.environ.get("K2_NO_TP"))
            NO_TAIL = bool(_os.environ.get("K2_NO_TAIL"))
            i2, i3 = qt % 2, qt % 3
            mask = m_q2 if qt == 2 else (m_q17 if qt == 17 else m_mid)
            pv_started = set()
            evpv_by_p = {}
            sc_ev = {}
            evscale_by_p = {}
            if pre:
                sc_ev[0] = pre["sc0"]
                evscale_by_p[0] = pre["ch0"]
            else:
                sc_ev[0] = SC(qt, 0, mask)
            sc_ev[1] = SC(qt, 1, mask)
            if 0 not in evscale_by_p:
                evscale_by_p[0] = CH(qt, 0, sc_ev[0])
            if 1 not in evscale_by_p:
                evscale_by_p[1] = CH(qt, 1, sc_ev[1])
            for p in range(NPAIRS):
                evtt = TP_t(qt, p, evscale_by_p[p])
                if p + 2 <= NPAIRS - 1:
                    sc_ev[p + 2] = SC(qt, p + 2, mask)
                    evscale_by_p[p + 2] = CH(qt, p + 2, sc_ev[p + 2])
                if p == 0:
                    stage_C1b()
                if p == 1 and interleave_t is not None:
                    A1_T(interleave_t, a1_elem_ev[interleave_t])
                if NO_TP:
                    continue
                evpv_by_p[p] = TP_pv(qt, p, evtt, pv_started)
                if p == 5:
                    W("vector", ch_ev_sums(evscale_by_p, 5), g["rsumg"])
                    O("vector", lambda E, o=rsum[:, 0:12], i=ssum[:, 0:12]:
                      E.reciprocal(o, i), "s_ve")
                    evrc1 = ("s_ve", pr.cnt["s_ve"], "vector")
                    pr.wait("vector", (evrc1[0], evrc1[1], "_bar"))
                    W("vector", g["delta"])
                    rb1 = bass.AP(rsum, 0, [[NH, P], [2, 6], [1, 2], [0, 64]])
                    evrcb1 = O("vector", lambda E, o=delta[:, 0:768], i=rb1:
                               E.tensor_copy(o, i), "s_ve")
                    for h in range(12):
                        g["ssum"][h] = evrcb1
                    W("tensor", evrcb1, g["t3"])
                    evsm1 = None
                    for pp in range(6):
                        evsm1 = O("tensor", lambda E,
                                  o=T3[:, P * pp:P * (pp + 1)],
                                  i=delta[:, P * pp:P * (pp + 1)],
                                  d=ident[:, :]: E.transpose(o, i, d),
                                  "s_pe" if pp == 5 else None)
                    W("vector", evsm1)
                    evse1 = O("vector", lambda E, o=s_sb[:, 0:768],
                              i=T3[:, 0:768]: E.tensor_copy(o, i), "s_ve")
                    g["t3"] = evse1
                    g["rsumg"] = evse1
                    g["delta"] = pr.later(g["delta"], evsm1)
                    oA = outfm[:, 0:512].rearrange("a (b k) -> a b k", k=P)
                    pA = P45[:, 0:512].rearrange("a (b k) -> a b k", k=P)
                    sA = s_sb[:, 0:512].rearrange("a (b k) -> a b k", k=P)
                    W("vector", evpv_by_p[3], evse1, g["outfm"])
                    evoeA = O("vector", lambda E, o=oA, a=pA, b=sA:
                              E.tensor_tensor(o, a, b, op=ALU.mult), "s_ve")
                if p == 6 and interleave_t is not None:
                    g["p12"] = pr.later(g["p12"], g["sc", (0, 0)],
                                        g["sc", (0, 1)])
                    mm_v(interleave_t)
            if NO_TP or NPAIRS < 8 or NO_TAIL:
                scores_done[qt] = sc_ev[NPAIRS - 1]
                pv_done[qt] = evpv_by_p.get(NPAIRS - 1, sc_ev[NPAIRS - 1])
                g["k"][(qt - 1) % 4] = scores_done[qt]
                g["v"][(qt - 1) % 4] = pv_done[qt]
                g["q"][i2] = scores_done[qt]
                return None
            scores_done[qt] = sc_ev[7]
            pv_done[qt] = evpv_by_p[7]
            W("vector", g["sc", (1, 1)], g["rsumg"])
            O("vector", lambda E, o=rsum[:, 12:16], i=ssum[:, 12:16]:
              E.reciprocal(o, i), "s_ve")
            evrc = ("s_ve", pr.cnt["s_ve"], "vector")
            pr.wait("vector", (evrc[0], evrc[1], "_bar"))
            rbview = bass.AP(rsum, 12, [[NH, P], [2, 2], [1, 2], [0, 64]])
            evrcb = O("vector", lambda E, o=delta[:, 768:1024], i=rbview:
                      E.tensor_copy(o, i), "s_ve")
            for h in range(12, NH):
                g["ssum"][h] = evrcb
            W("tensor", evrcb, g["t3"])
            evsm = None
            for pp in range(6, NCH):
                evsm = O("tensor", lambda E, o=T3[:, P * pp:P * (pp + 1)],
                         i=delta[:, P * pp:P * (pp + 1)], d=ident[:, :]:
                         E.transpose(o, i, d), "s_pe" if pp == NCH - 1 else None)
            W("vector", evsm)
            evse = O("vector", lambda E, o=s_sb[:, 768:1024], i=T3[:, 768:1024]:
                     E.tensor_copy(o, i), "s_ve")
            g["t3"] = evse
            g["rsumg"] = evse
            g["delta"] = pr.later(g["delta"], evsm)
            oB = outfm[:, 512:1024].rearrange("a (b k) -> a b k", k=P)
            pB = P45[:, 512:1024].rearrange("a (b k) -> a b k", k=P)
            sB = s_sb[:, 512:1024].rearrange("a (b k) -> a b k", k=P)
            W("vector", evpv_by_p[7], evse, g["outfm"])
            evoeB = O("vector", lambda E, o=oB, a=pB, b=sB:
                      E.tensor_tensor(o, a, b, op=ALU.mult), "s_ve")
            g["p45"] = evoeB
            g["k"][(qt - 1) % 4] = scores_done[qt]
            g["v"][(qt - 1) % 4] = pv_done[qt]
            g["q"][i2] = scores_done[qt]

            W("tensor", evoeA, wdef["wo"], g["p67"],
              g["sc", (1, 0)], g["sc", (1, 1)])
            evap = None
            for c in range(NCH):
                if c == 4:
                    W("tensor", evoeB)
                for b in range(2):
                    evap = O("tensor", lambda E,
                             o=P67[:, 512 * b:512 * (b + 1)],
                             l=outfm[:, P * c:P * (c + 1)],
                             rh=wout[:, D * c + 512 * b:D * c + 512 * (b + 1)],
                             st=(c == 0), sp=(c == NCH - 1):
                             E.matmul(o, l, rh, start=st, stop=sp),
                             "s_pe" if (b == 1 and c == NCH - 1) else None)
            g["outfm"] = evap

            W("vector", evap, g["a2buf"], g["stat2"][i2])
            eva2 = O("vector", lambda E, o=a2buf[:, :], i=x_in[i3][:, :],
                     s=C(TWO), pp=P67[:, :], a=s2s[i2][:, :]:
                     E.scalar_tensor_tensor(o, i, s, pp, op0=ALU.mult,
                                            op1=ALU.add, accum_out=a), "s_ve")
            a2ev[qt] = eva2
            g["p67"] = eva2
            g["x_in"][i3] = pr.later(g["x_in"][i3], eva2)

            evsq, evr = stats(a2buf[:, :], s2s[i2], e22[i2], m2[i2], p2s[i2],
                              r2[i2], eva2, extra=g["stat2"][i2])
            W("scalar", evr)
            evmr = O("scalar", lambda E, o=lnvs[:, :], i=m2[i2][:, :],
                     s=r2[i2][:, :]: E.activation(o, i, AF.Copy, scale=s),
                     "s_ae")
            W("vector", evr, g["y_tm"][i2])
            evy = O("vector", lambda E, o=y_tm[i2][:, :], i=a2buf[:, :],
                    a=m2[i2][:, :], b=r2[i2][:, :]: E.tensor_scalar(
                o, i, a, b, op0=ALU.add, op1=ALU.mult), "s_ve")
            g["stat2"][i2] = evy
            W("scalar", evmr, g["ybf"])
            evyb = O("scalar", lambda E, o=ybf[:, :], i=a2buf[:, :],
                     s=r2[i2][:, :], b=lnvs[:, :]: E.activation(
                o, i, AF.Identity, scale=s, bias=b), "s_ae")
            g["a2buf"] = pr.later(evy, evsq, evyb)
            g["stat2"][i2] = pr.later(evy, evmr, evyb)
            return evyb

        def stage_B_tail(qt, evyb, interleave_t):
            if evyb is None:
                return
            i2 = qt % 2
            if interleave_t is not None:
                mm_q(interleave_t)
            W("tensor", evyb, g["t0"])
            evyt = None
            for c in range(NCH):
                evyt = O("tensor", lambda E, o=T0[:, P * c:P * (c + 1)],
                         i=ybf[:, P * c:P * (c + 1)], d=ident[:, :]:
                         E.transpose(o, i, d), "s_pe" if c == NCH - 1 else None)
            g["ybf"] = evyt
            W("scalar", evyt, g["y_fm"][i2])
            yview = y_fm[i2][:, :].rearrange("a (c k) -> a c k", k=130)[:, :, 2:130]
            t0view = T0[:, :].rearrange("a (c k) -> a c k", k=P)
            evye = O("scalar", lambda E, o=yview, i=t0view:
                     E.activation(o, i, AF.Copy), "s_ae")
            yevac[qt] = evye
            g["t0"] = evye
            if interleave_t is not None:
                mm_k(interleave_t)

        c2_pend = []
        c1b_pend = []

        def stage_C1b():
            if not c1b_pend:
                return
            ct, evr_, evks = c1b_pend.pop(0)
            i2 = ct % 2
            W("scalar", evr_, g["sig"])
            eve1 = O("scalar", lambda E, o=sig[:, :], i=P45[:, :], s=C(NEGONE):
                     E.activation(o, i, AF.Exp, scale=s), "s_ae")
            W("scalar", eve1, g["ae_scr"])
            evsp = O("scalar", lambda E, o=ae_scr[:, :], i=sig[:, :], b=C(ONE):
                     E.activation(o, i, AF.Ln, bias=b), "s_ae")
            W("scalar", evsp)
            evsg = O("scalar", lambda E, o=sig[:, :], i=ae_scr[:, :],
                     s=C(NEGONE): E.activation(o, i, AF.Exp, scale=s), "s_ae")
            g["ae_scr"] = evsg
            W("tensor", evks, wdef["wv"], eve1, g["p45"])
            evvo = None
            for b in range(2):
                for c in range(NCH):
                    evvo = O("tensor", lambda E,
                             o=P45[:, 512 * b:512 * (b + 1)],
                             l=kact[:, P * c:P * (c + 1)],
                             rh=wv[:, D * c + 512 * b:D * c + 512 * (b + 1)],
                             st=(c == 0), sp=(c == NCH - 1):
                             E.matmul(o, l, rh, start=st, stop=sp),
                             "s_pe" if (b == 1 and c == NCH - 1) else None)
            g["kact"] = evvo
            W("vector", evvo, evsg)
            O("vector", lambda E, o=sig[:, :], i=P45[:, :], s=C(ZERO):
              E.scalar_tensor_tensor(o, i, s, o, op0=ALU.add, op1=ALU.mult),
              "s_ve")
            evt1 = ("s_ve", pr.cnt["s_ve"], "vector")
            g["p45"] = evt1
            g["sig"] = evt1
            c2_pend.append((ct, evt1))

        def stage_C2():
            if not c2_pend:
                return
            ct, evt1 = c2_pend.pop(0)
            i2 = ct % 2
            W("vector", evt1, g["stat3"][i2])
            eva3 = O("vector", lambda E, o=sig[:, :], i=sig[:, :], s=C(ZERO),
                     t=y_tm[i2][:, :], a=s3s[i2][:, :]:
                     E.scalar_tensor_tensor(o, i, s, t, op0=ALU.add,
                                            op1=ALU.add, accum_out=a), "s_ve")
            g["sig"] = eva3
            g["y_tm"][i2] = pr.later(g["y_tm"][i2], eva3)
            evsq, evr = stats(sig[:, :], s3s[i2], e23[i2], m3[i2], p3s[i2],
                              r3[i2], eva3, extra=g["stat3"][i2])
            W("vector", evr, g["out_sb"][i2])
            evout = O("vector", lambda E, o=out_sb[i2][:, :], i=sig[:, :],
                      a=m3[i2][:, :], b=r3[i2][:, :]: E.tensor_scalar(
                o, i, a, b, op0=ALU.add, op1=ALU.mult), "s_ve")
            g["stat3"][i2] = evout
            g["sig"] = pr.later(g["sig"], evout, evsq)
            W("gpsimd", evout)
            evst = dma("gpsimd", f"s_st{i2}",
                       out_d[P * (ct - CT0):P * (ct - CT0 + 1), :],
                       out_sb[i2][:, :])
            g["out_sb"][i2] = evst

        def stage_C(ct):
            i2, pp2 = ct % 2, (ct - 1) % 2

            W("vector", yevac[ct], yevac[ct - 1], g["y_fm"][i2], ev_cv, ev_c)
            evpc = None
            for c in range(NCH):
                if ct == CT0:
                    evpc = O("vector", lambda E,
                             o=y_fm[i2][:, 130 * c + 1:130 * c + 2],
                             i=y_fm[pp2][:, 130 * c + 129:130 * c + 130],
                             s=gate[:, :]: E.tensor_scalar(
                        o, i, s, None, op0=ALU.mult),
                        "s_ve" if c == NCH - 1 else None)
                else:
                    evpc = O("vector", lambda E,
                             o=y_fm[i2][:, 130 * c + 1:130 * c + 2],
                             i=y_fm[pp2][:, 130 * c + 129:130 * c + 130]:
                             E.tensor_copy(o, i), "s_ve" if c == NCH - 1 else None)
            g["y_fm"][pp2] = pr.later(g["y_fm"][pp2], evpc)

            own = y_fm[i2][:, :].rearrange("a (c k) -> a c k", k=130)[:, :, 2:130]
            prev = y_fm[i2][:, :].rearrange("a (c k) -> a c k", k=130)[:, :, 1:129]
            dview = delta[:, :].rearrange("a (c k) -> a c k", k=P)

            W("vector", evpc, g["delta"])
            evd = O("vector", lambda E, o=dview, a=own, b=prev:
                    E.tensor_tensor(o, a, b, op=ALU.subtract), "s_ve")
            W("vector", evd, g["xr"])
            evxr = None
            for c in range(NCH):
                evxr = O("vector", lambda E, o=xr_fm[:, P * c:P * (c + 1)],
                         i=delta[:, P * c:P * (c + 1)], s=mur[:, c:c + 1],
                         t=y_fm[i2][:, 130 * c + 1:130 * c + 129]:
                         E.scalar_tensor_tensor(o, i, s, t, op0=ALU.mult,
                                                op1=ALU.add),
                         "s_ve" if c == NCH - 1 else None)
            W("vector", evd, g["xk"])
            evxk = None
            for c in range(NCH):
                evxk = O("vector", lambda E, o=xk_fm[:, P * c:P * (c + 1)],
                         i=delta[:, P * c:P * (c + 1)], s=muk[:, c:c + 1],
                         t=y_fm[i2][:, 130 * c + 1:130 * c + 129]:
                         E.scalar_tensor_tensor(o, i, s, t, op0=ALU.mult,
                                                op1=ALU.add),
                         "s_ve" if c == NCH - 1 else None)
            g["delta"] = pr.later(evxr, evxk)
            g["y_fm"][i2] = pr.later(g["y_fm"][i2], evxr, evxk)

            W("tensor", evxr, wdef["wr"], g["p45"])
            evr_ = None
            for b in range(2):
                for c in range(NCH):
                    evr_ = O("tensor", lambda E,
                             o=P45[:, 512 * b:512 * (b + 1)],
                             l=xr_fm[:, P * c:P * (c + 1)],
                             rh=wr[:, D * c + 512 * b:D * c + 512 * (b + 1)],
                             st=(c == 0), sp=(c == NCH - 1):
                             E.matmul(o, l, rh, start=st, stop=sp),
                             "s_pe" if (b == 1 and c == NCH - 1) else None)
            g["xr"] = evr_

            W("tensor", evxk, wdef["wk"], g["p67"])
            evkf = None
            for r_ in range(NCH):
                for c in range(NCH):
                    evkf = O("tensor", lambda E,
                             o=P67[:, P * r_:P * (r_ + 1)],
                             l=wk[:, D * c + P * r_:D * c + P * (r_ + 1)],
                             rh=xk_fm[:, P * c:P * (c + 1)],
                             st=(r_ % 4 == 0 and c == 0),
                             sp=(r_ % 4 == 3 and c == NCH - 1):
                             E.matmul(o, l, rh, start=st, stop=sp),
                             "s_pe" if (r_ == NCH - 1 and c == NCH - 1) else None)
            g["xk"] = evkf

            W("scalar", evkf, g["krelu"], g["delta"])
            evkr = O("scalar", lambda E, o=krelu[:, :], i=P67[:, :]:
                     E.activation(o, i, AF.Relu), "s_ae")
            g["p67"] = evkr
            g["sc", (1, 0)] = evkr
            g["sc", (1, 1)] = evkr
            W("vector", evkr, g["kact"])
            evks = O("vector", lambda E, o=kact[:, :], i=krelu[:, :]:
                     E.tensor_mul(o, i, i), "s_ve")
            g["krelu"] = evks
            g["delta"] = pr.later(g["delta"], evks)

            c1b_pend.append((ct, evr_, evks))

        a1_elem_ev = {}
        for t in range(3):
            a1_elem_ev[t] = A1_elem(t)
            A1_T(t, a1_elem_ev[t])
            if QT0 <= t <= QT1:
                mm_q(t)
            mm_k(t)
            mm_v(t)
        for c in range(NCH):
            wdef["wo"] = dma("gpsimd", "s_wo", wout[:, D * c:D * (c + 1)],
                             wout_d[P * c:P * (c + 1), :])
        for c in range(NCH):
            wdef["wr"] = dma("gpsimd", "s_wr", wr[:, D * c:D * (c + 1)],
                             wr_d[P * c:P * (c + 1), :])
        for c in range(NCH):
            wdef["wk"] = dma("gpsimd", "s_wk", wk[:, D * c:D * (c + 1)],
                             wk_d[P * c:P * (c + 1), :])
        for c in range(NCH):
            wdef["wv"] = dma("gpsimd", "s_wv", wv[:, D * c:D * (c + 1)],
                             wv_d[P * c:P * (c + 1), :])

        import os as _os
        ILV = not _os.environ.get("K2_NO_ILV")
        MAXT = int(_os.environ.get("K2_MAXT", NT - 1))
        pre_b = None
        for t in range(1, MAXT):
            nxt = t + 2 if t + 2 <= NT - 1 else None
            if nxt is not None:
                W("sync", g["x_in"][nxt % 3])
                ld_ev[nxt] = dma("sync", f"s_x{nxt % 3}", x_in[nxt % 3][:, :],
                                 x_d[P * nxt:P * (nxt + 1), :])
                a1_elem_ev[nxt] = A1_elem(nxt)
            if not ILV and nxt is not None:
                A1_T(nxt, a1_elem_ev[nxt])
                mm_v(nxt)
                mm_q(nxt)
                mm_k(nxt)
            evyb = stage_B(t, interleave_t=(nxt if ILV else None), pre=pre_b)
            stage_B_tail(t, evyb, (nxt if ILV else None))
            pre_b = None
            if t + 1 < MAXT:
                mask_n = m_q2 if t + 1 == 2 else (m_q17 if t + 1 == 17 else m_mid)
                sc0 = SC(t + 1, 0, mask_n)
                ch0 = CH(t + 1, 0, sc0)
                pre_b = {"sc0": sc0, "ch0": ch0}
            stage_C2()
            if t >= CT0:
                stage_C(t)

        stage_C1b()
        stage_C2()
        stage_C2()
        if pr.cnt.get("s_st0"):
            W("gpsimd", ("s_st0", pr.cnt["s_st0"], "_dma"))
        if pr.cnt.get("s_st1"):
            W("gpsimd", ("s_st1", pr.cnt["s_st1"], "_dma"))

        with nc.Block() as block:
            def make_runner(name):
                def run(E):
                    for item in pr.ops[name]:
                        if item[0] == "w":
                            E.wait_ge(sems[item[1]], item[2])
                        else:
                            ins = item[1](E)
                            if item[2] is not None:
                                ins.then_inc(sems[item[2]], item[3])
                return run

            block.sync(make_runner("sync"))
            block.vector(make_runner("vector"))
            block.scalar(make_runner("scalar"))
            block.tensor(make_runner("tensor"))
            block.gpsimd(make_runner("gpsimd"))

    return nc


_CACHE = {}


def _get_nc():
    if "nc" not in _CACHE:
        _CACHE["nc"] = build_program()
    return _CACHE["nc"]


def _build_masks(h):
    i = np.arange(P)[:, None]
    j = np.arange(3 * P)[None, :]
    band = (j - i >= 0) & (j - i <= 256)
    NEG = -1e9
    mid = np.where(band, 0.0, NEG).astype(np.float32)
    q2 = np.where(band & (j >= 128), 0.0, NEG).astype(np.float32) if h == 0 else mid
    q17 = np.where(band & (j < 256), 0.0, NEG).astype(np.float32) if h == 1 else mid
    perm = np.r_[0:128, 256:384, 128:256]
    bf = ml_dtypes.bfloat16
    return (mid[:, perm].astype(bf), q2[:, perm].astype(bf),
            q17[:, perm].astype(bf))


def kernel(L0_attn_x, L0_attn_w_qkv, L0_attn_w_out, L0_attn_gamma, L0_attn_beta,
           L0_ffn_W_r, L0_ffn_W_k, L0_ffn_W_v, L0_ffn_mu_r, L0_ffn_mu_k):
    x = np.asarray(L0_attn_x, dtype=np.float32)
    wqkvT = np.ascontiguousarray(np.asarray(L0_attn_w_qkv, dtype=np.float32).T)
    woutT = np.ascontiguousarray(np.asarray(L0_attn_w_out, dtype=np.float32).T)
    wrT = np.ascontiguousarray(np.asarray(L0_ffn_W_r, dtype=np.float32).T)
    wkT = np.ascontiguousarray(np.asarray(L0_ffn_W_k, dtype=np.float32).T)
    wvT = np.ascontiguousarray(np.asarray(L0_ffn_W_v, dtype=np.float32).T)

    def fm(vec):
        return np.ascontiguousarray(
            np.asarray(vec, dtype=np.float32).reshape(NCH, P).T)

    gamma_fm, beta_fm = fm(L0_attn_gamma), fm(L0_attn_beta)
    mur_fm, muk_fm = fm(L0_ffn_mu_r), fm(L0_ffn_mu_k)
    ident = np.eye(P, dtype=np.float32).astype(ml_dtypes.bfloat16)

    in_maps = []
    for core in range(8):
        b, h = core // 2, core % 2
        s = h * 2048
        xp = np.zeros((NT * P, D), np.float32)
        lo, hi = s - 256, s + 2176
        clo, chi = max(lo, 0), min(hi, 4096)
        xp[clo - lo:chi - lo] = x[b, clo:chi]
        mid, q2, q17 = _build_masks(h)
        in_maps.append({
            "x": xp, "wqkvT": wqkvT, "woutT": woutT, "wrT": wrT, "wkT": wkT,
            "wvT": wvT, "gamma_fm": gamma_fm, "beta_fm": beta_fm,
            "mur_fm": mur_fm, "muk_fm": muk_fm,
            "gate": np.full((P, 1), 0.0 if h == 0 else 1.0, np.float32),
            "ident": ident, "mask_mid": mid, "mask_q2": q2, "mask_q17": q17,
        })

    nc = _get_nc()
    res = run_bass_kernel_spmd(nc, in_maps, core_ids=list(range(8)))
    out = np.empty((4, 4096, D), np.float32)
    for core in range(8):
        b, h = core // 2, core % 2
        out[b, h * 2048:(h + 1) * 2048] = res.results[core]["out"]
    return out


# version 27 (final)
# speedup vs baseline: 1.1039x; 1.0821x over previous
from contextlib import ExitStack
import os
import numpy as np
import ml_dtypes
import concourse.bass as bass
import concourse.mybir as mybir
from concourse.bass_utils import run_bass_kernel_spmd

F32 = mybir.dt.float32
BF16 = mybir.dt.bfloat16
AF = mybir.ActivationFunctionType
ALU = mybir.AluOpType
AX = mybir.AxisListType

P = 128
D = 1024
NCH = 8
NT = 19
QT0, QT1 = 1, 17
CT0, CT1 = 2, 17
NH = 16
HD = 64

ENGS = ("sync", "vector", "scalar", "tensor", "gpsimd")


class Prog:
    def __init__(self, sems):
        self.sems = sems
        self.ops = {e: [] for e in ENGS}
        self.cnt = {}
        self.waited = {e: {} for e in ENGS}

    def wait(self, eng, ev):
        if ev is None:
            return
        if isinstance(ev, list):
            for x in ev:
                self.wait(eng, x)
            return
        sem, val, src = ev
        if self.waited[eng].get(sem, -1) >= val:
            return
        self.waited[eng][sem] = val
        self.ops[eng].append(("w", sem, val))

    def do(self, eng, fn, inc=None, n=1):
        if inc is None:
            self.ops[eng].append(("o", fn, None, 0))
            return None
        self.cnt[inc] = self.cnt.get(inc, 0) + n
        self.ops[eng].append(("o", fn, inc, n))
        return (inc, self.cnt[inc], eng)

    def later(self, *evs):
        best = {}
        for ev in evs:
            if ev is None:
                continue
            if isinstance(ev, list):
                for x in ev:
                    if x is not None and (x[0] not in best or best[x[0]][1] < x[1]):
                        best[x[0]] = x
                continue
            if ev[0] not in best or best[ev[0]][1] < ev[1]:
                best[ev[0]] = ev
        return list(best.values())


def build_program():
    nc = bass.Bass("TRN2", target_bir_lowering=False)

    x_d = nc.dram_tensor("x", [NT * P, D], F32, kind="ExternalInput")
    wqkv_d = nc.dram_tensor("wqkvT", [D, 3 * D], F32, kind="ExternalInput")
    wout_d = nc.dram_tensor("woutT", [D, D], F32, kind="ExternalInput")
    wr_d = nc.dram_tensor("wrT", [D, D], F32, kind="ExternalInput")
    wk_d = nc.dram_tensor("wkT", [D, D], F32, kind="ExternalInput")
    wv_d = nc.dram_tensor("wvT", [D, D], F32, kind="ExternalInput")
    gam_d = nc.dram_tensor("gamma_fm", [P, NCH], F32, kind="ExternalInput")
    bet_d = nc.dram_tensor("beta_fm", [P, NCH], F32, kind="ExternalInput")
    mur_d = nc.dram_tensor("mur_fm", [P, NCH], F32, kind="ExternalInput")
    muk_d = nc.dram_tensor("muk_fm", [P, NCH], F32, kind="ExternalInput")
    gate_d = nc.dram_tensor("gate", [P, 1], F32, kind="ExternalInput")
    id_d = nc.dram_tensor("ident", [P, P], BF16, kind="ExternalInput")
    mmid_d = nc.dram_tensor("mask_mid", [P, 384], BF16, kind="ExternalInput")
    mq2_d = nc.dram_tensor("mask_q2", [P, 384], BF16, kind="ExternalInput")
    mq17_d = nc.dram_tensor("mask_q17", [P, 384], BF16, kind="ExternalInput")
    out_d = nc.dram_tensor("out", [16 * P, D], F32, kind="ExternalOutput")

    with ExitStack() as ctx:
        e = ctx.enter_context

        wqkv = e(nc.sbuf_tensor("wqkv", [P, NCH * 3 * D], BF16))
        wout = e(nc.sbuf_tensor("wout", [P, NCH * D], BF16))
        wr = e(nc.sbuf_tensor("wr", [P, NCH * D], BF16))
        wk = e(nc.sbuf_tensor("wk", [P, NCH * D], BF16))
        wv = e(nc.sbuf_tensor("wv", [P, NCH * D], BF16))
        gam = e(nc.sbuf_tensor("gam", [P, NCH], F32))
        bet = e(nc.sbuf_tensor("bet", [P, NCH], F32))
        mur = e(nc.sbuf_tensor("mur", [P, NCH], F32))
        muk = e(nc.sbuf_tensor("muk", [P, NCH], F32))
        gate = e(nc.sbuf_tensor("gate_t", [P, 1], F32))
        ident = e(nc.sbuf_tensor("ident_t", [P, P], BF16))
        m_mid = e(nc.sbuf_tensor("m_mid", [P, 384], BF16))
        m_q2 = e(nc.sbuf_tensor("m_q2", [P, 384], BF16))
        m_q17 = e(nc.sbuf_tensor("m_q17", [P, 384], BF16))
        s_sb = e(nc.sbuf_tensor("s_sb", [P, D], BF16))
        consts = e(nc.sbuf_tensor("consts", [P, 10], F32))
        EPS, NHALF, INV_D, EIGHTH, ZERO, TWO, ONE, NEGONE, INV_SQD, NINV_D = range(10)

        x_in = [e(nc.sbuf_tensor(f"x_in{i}", [P, D], F32)) for i in range(3)]
        xnz = [e(nc.sbuf_tensor(f"xnz{i}", [P, D], BF16)) for i in range(2)]
        xn_fm = [e(nc.sbuf_tensor(f"xn_fm{i}", [P, D], BF16)) for i in range(2)]
        q_sb = [e(nc.sbuf_tensor(f"q_sb{i}", [P, D], BF16)) for i in range(2)]
        k_sb = [e(nc.sbuf_tensor(f"k_sb{i}", [P, D], BF16)) for i in range(4)]
        v_sb = [e(nc.sbuf_tensor(f"v_sb{i}", [P, D], BF16)) for i in range(4)]
        probs = [e(nc.sbuf_tensor(f"probs{i}", [P, 384], BF16)) for i in range(4)]
        probsT = e(nc.sbuf_tensor("probsT", [P, 768], BF16))
        outfm = e(nc.sbuf_tensor("outfm", [P, D], BF16))
        a2buf = e(nc.sbuf_tensor("a2buf", [P, D], F32))
        y_tm = [e(nc.sbuf_tensor(f"y_tm{i}", [P, D], F32)) for i in range(2)]
        ybf = e(nc.sbuf_tensor("ybf", [P, D], BF16))
        y_fm = [e(nc.sbuf_tensor(f"y_fm{i}", [P, NCH * 130], BF16)) for i in range(2)]
        delta = e(nc.sbuf_tensor("delta", [P, D], BF16))
        xr_fm = e(nc.sbuf_tensor("xr_fm", [P, D], BF16))
        xk_fm = e(nc.sbuf_tensor("xk_fm", [P, D], BF16))
        kact = e(nc.sbuf_tensor("kact", [P, D], BF16))
        krelu = delta
        sig = e(nc.sbuf_tensor("sig", [P, D], F32))
        ae_scr = e(nc.sbuf_tensor("ae_scr", [P, D], F32))
        out_sb = [e(nc.sbuf_tensor(f"out_sb{i}", [P, D], F32)) for i in range(2)]

        def slots(name, n=2):
            return [e(nc.sbuf_tensor(f"{name}{i}", [P, 1], F32)) for i in range(n)]
        sx1, e21, m1, p1s, r1 = slots("sx1"), slots("e21"), slots("m1"), slots("p1sl"), slots("r1")
        s2s, e22, m2, p2s, r2 = slots("s2s"), slots("e22"), slots("m2"), slots("p2s"), slots("r2")
        s3s, e23, m3, p3s, r3 = slots("s3s"), slots("e23"), slots("m3"), slots("p3s"), slots("r3")
        nm2s = slots("nm2s", 1)[0]
        lnvs = slots("lnvs", 1)[0]
        nmax2 = [e(nc.sbuf_tensor(f"nmax2_{i}", [P, 2], F32)) for i in range(2)]
        ssum = e(nc.sbuf_tensor("ssum", [P, NH], F32))
        rsum = e(nc.sbuf_tensor("rsum", [P, NH], F32))

        T0 = e(nc.psum_tensor("T0", [P, 1024], BF16))
        T3 = e(nc.psum_tensor("T3", [P, 1024], BF16))
        P12 = e(nc.psum_tensor("P12", [P, 1024], F32))
        P45 = e(nc.psum_tensor("P45", [P, 1024], F32))
        P67 = e(nc.psum_tensor("P67", [P, 1024], F32))

        sems = {}
        for s in ("s_c", "s_x0", "s_x1", "s_x2", "s_wq", "s_wo", "s_wr",
                  "s_wk", "s_wv", "s_st0", "s_st1",
                  "s_pe", "s_ve", "s_ae", "s_gp",
                  *[f"s_wq{i}" for i in range(NCH)]):
            sems[s] = e(nc.semaphore(s))

        pr = Prog(sems)

        def W(eng, *evs):
            for ev in evs:
                pr.wait(eng, ev)

        O = pr.do

        def C(j):
            return consts[:, j:j + 1]

        def dma(eng, sem, out_ap, in_ap):
            cnt = pr.cnt.get(sem, 0) + 16
            pr.cnt[sem] = cnt
            pr.ops[eng].append(
                ("o", lambda E, o=out_ap, i=in_ap: E.dma_start(o, i), sem, 16))
            return (sem, cnt, "_dma")

        ld_ev = {}
        ev = None
        for dst, src in ((gam, gam_d), (bet, bet_d), (mur, mur_d), (muk, muk_d),
                         (gate, gate_d), (ident, id_d), (m_mid, mmid_d),
                         (m_q2, mq2_d), (m_q17, mq17_d)):
            ev = dma("sync", "s_c", dst[:, :], src[:, :])
        ev_c = ev

        evm = None
        for j, val in ((EPS, 1e-5), (NHALF, -0.5), (INV_D, 1.0 / D),
                       (EIGHTH, 0.125), (ZERO, 0.0), (TWO, 2.0), (ONE, 1.0),
                       (NEGONE, -1.0), (INV_SQD, 1.0 / 32.0),
                       (NINV_D, -1.0 / D)):
            evm = O("vector", lambda E, jj=j, vv=val: E.memset(
                consts[:, jj:jj + 1], vv), "s_ve")
        ev_cv = evm

        ev_wq_c = []
        for c in range(NCH):
            ev_wq_c.append(dma("gpsimd", f"s_wq{c}",
                               wqkv[:, 3 * D * c:3 * D * (c + 1)],
                               wqkv_d[P * c:P * (c + 1), :]))
        ev_wq = ev_wq_c[-1]
        wdef = {}

        for t in range(3):
            ld_ev[t] = dma("sync", f"s_x{t % 3}", x_in[t % 3][:, :],
                           x_d[P * t:P * (t + 1), :])

        g = {
            "t0": None, "t3": None,
            "p12": None, "p45": None, "p67": None,
            "xn_fm": [None, None], "q": [None, None],
            "k": [None] * 4, "v": [None] * 4,
            "x_in": [None] * 3, "xnz": [None, None],
            "pm": [None] * 4, "probsT": [None, None],
            "a2buf": None, "y_tm": [None, None], "y_fm": [None, None],
            "out_sb": [None, None],
            "stat1": [None, None], "stat2": [None, None], "stat3": [None, None],
            "nmax": [None, None], "ssum": [None] * NH, "rsumg": None,
            "ybf": None, "outfm": None, "delta": None, "ae_scr": None,
            ("probs_buf", 0): None, ("probs_buf", 1): None,
            ("probs_buf", 2): None, ("probs_buf", 3): None,
            "xr": None, "xk": None, "krelu": None, "kact": None, "sig": None,
        }
        for par in (0, 1):
            for hp in (0, 1):
                g["sc", (par, hp)] = None
        kevac, vevac, qevac = {}, {}, {}
        scores_done, pv_done, yevac, a2ev = {}, {}, {}, {}
        xnev = {}

        def stats(src_ap, sumslot, e2slot, mslot, epslot, rslot, evsum, extra=None):
            W("scalar", evsum, ev_cv, extra, g["ae_scr"])
            evsq = O("scalar", lambda E, o=ae_scr[:, :], i=src_ap,
                     a=e2slot[:, :], s=C(INV_SQD): E.activation(
                o, i, AF.Square, scale=s, accum_out=a), "s_ae")
            W("scalar", evsq)
            ev1 = O("scalar", lambda E, o=epslot[:, :], i=e2slot[:, :],
                    b=C(EPS): E.activation(o, i, AF.Identity, bias=b), "s_ae")
            ev2 = O("scalar", lambda E, o=mslot[:, :], i=sumslot[:, :],
                    s=C(NINV_D): E.activation(o, i, AF.Copy, scale=s), "s_ae")
            W("scalar", ev2)
            ev3 = O("scalar", lambda E, o=nm2s[:, :], i=mslot[:, :]:
                    E.activation(o, i, AF.Square), "s_ae")
            W("scalar", ev1, ev3)
            ev4 = O("scalar", lambda E, o=lnvs[:, :], i=nm2s[:, :], s=C(NEGONE),
                    b=epslot[:, :]: E.activation(o, i, AF.Ln, scale=s, bias=b),
                    "s_ae")
            W("scalar", ev4)
            evr = O("scalar", lambda E, o=rslot[:, :], i=lnvs[:, :],
                    s=C(NHALF): E.activation(o, i, AF.Exp, scale=s), "s_ae")
            g["ae_scr"] = evsq
            return evsq, evr

        def A1_elem(t):
            i3, i2 = t % 3, t % 2
            xin = x_in[i3]
            W("vector", ld_ev[t], ev_cv, g["stat1"][i2])
            evsum = O("vector", lambda E, o=sx1[i2][:, :], i=xin[:, :]:
                      E.tensor_reduce(o, i, axis=AX.X, op=ALU.add), "s_ve")
            evsq, evr = stats(xin[:, :], sx1[i2], e21[i2], m1[i2], p1s[i2],
                              r1[i2], evsum, extra=pr.later(ld_ev[t], g["stat1"][i2]))
            W("vector", evr, g["xnz"][i2])
            evxnz = O("vector", lambda E, o=xnz[i2][:, :], i=xin[:, :],
                      a=m1[i2][:, :], b=r1[i2][:, :]: E.tensor_scalar(
                o, i, a, b, op0=ALU.add, op1=ALU.mult), "s_ve")
            g["stat1"][i2] = evxnz
            g["x_in"][i3] = pr.later(evxnz, evsq)
            return evxnz

        def A1_T(t, evxnz):
            i2 = t % 2
            W("tensor", evxnz, g["t0"], ev_c)
            evtp = None
            for c in range(NCH):
                evtp = O("tensor", lambda E, o=T0[:, P * c:P * (c + 1)],
                         i=xnz[i2][:, P * c:P * (c + 1)], d=ident[:, :]:
                         E.transpose(o, i, d), "s_pe" if c == NCH - 1 else None)
            g["xnz"][i2] = evtp
            W("scalar", evtp, ev_c, g["xn_fm"][i2])
            evxe = None
            for c in range(NCH):
                evxe = O("scalar", lambda E, o=xn_fm[i2][:, P * c:P * (c + 1)],
                         i=T0[:, P * c:P * (c + 1)], a=gam[:, c:c + 1],
                         b=bet[:, c:c + 1]: E.activation(
                    o, i, AF.Identity, scale=a, bias=b),
                    "s_ae" if c == NCH - 1 else None)
            g["t0"] = evxe
            xnev[t] = evxe
            return evxe

        def mm_q(t):
            i2 = t % 2
            W("tensor", xnev[t], g["p45"], g["q"][i2])
            last = None
            for r in range(NCH):
                for c in range(NCH):
                    W("tensor", ev_wq_c[c])
                    last = O("tensor", lambda E,
                             o=P45[:, P * r:P * (r + 1)],
                             l=wqkv[:, 3 * D * c + P * r:3 * D * c + P * (r + 1)],
                             rh=xn_fm[i2][:, P * c:P * (c + 1)],
                             st=(r % 4 == 0 and c == 0),
                             sp=(r % 4 == 3 and c == NCH - 1):
                             E.matmul(o, l, rh, start=st, stop=sp),
                             "s_pe" if (r == NCH - 1 and c == NCH - 1) else None)
            W("vector", last, g["q"][i2])
            evq = None
            for b in range(2):
                evq = O("vector", lambda E, o=q_sb[i2][:, 512 * b:512 * (b + 1)],
                        i=P45[:, 512 * b:512 * (b + 1)], s=C(EIGHTH):
                        E.tensor_scalar(o, i, s, None, op0=ALU.mult),
                        "s_ve" if b == 1 else None)
            g["p45"] = evq
            qevac[t] = evq
            g["xn_fm_q", t] = last
            return last

        def mm_k(t):
            i2, i4 = t % 2, t % 4
            W("tensor", xnev[t], g["p67"])
            last = None
            for r in range(NCH):
                for c in range(NCH):
                    W("tensor", ev_wq_c[c])
                    last = O("tensor", lambda E,
                             o=P67[:, P * r:P * (r + 1)],
                             l=wqkv[:, 3 * D * c + D + P * r:3 * D * c + D + P * (r + 1)],
                             rh=xn_fm[i2][:, P * c:P * (c + 1)],
                             st=(r % 4 == 0 and c == 0),
                             sp=(r % 4 == 3 and c == NCH - 1):
                             E.matmul(o, l, rh, start=st, stop=sp),
                             "s_pe" if (r == NCH - 1 and c == NCH - 1) else None)
            W("scalar", last, g["k"][i4])
            evke = None
            for b in range(2):
                evke = O("scalar", lambda E, o=k_sb[i4][:, 512 * b:512 * (b + 1)],
                         i=P67[:, 512 * b:512 * (b + 1)]:
                         E.activation(o, i, AF.Copy), "s_ae" if b == 1 else None)
            g["p67"] = evke
            g["sc", (1, 0)] = pr.later(g["sc", (1, 0)], evke)
            g["sc", (1, 1)] = pr.later(g["sc", (1, 1)], evke)
            kevac[t] = evke
            return last

        def mm_v(t):
            i2, i4 = t % 2, t % 4
            W("tensor", xnev[t], g["p12"])
            last = None
            for b in range(2):
                for c in range(NCH):
                    W("tensor", ev_wq_c[c])
                    last = O("tensor", lambda E,
                             o=P12[:, 512 * b:512 * (b + 1)],
                             l=xn_fm[i2][:, P * c:P * (c + 1)],
                             rh=wqkv[:, 3 * D * c + 2 * D + 512 * b:
                                     3 * D * c + 2 * D + 512 * (b + 1)],
                             st=(c == 0), sp=(c == NCH - 1):
                             E.matmul(o, l, rh, start=st, stop=sp),
                             "s_pe" if (b == 1 and c == NCH - 1) else None)
            W("vector", last, g["v"][i4])
            evve = None
            for b in range(2):
                evve = O("vector", lambda E, o=v_sb[i4][:, 512 * b:512 * (b + 1)],
                         i=P12[:, 512 * b:512 * (b + 1)]:
                         E.tensor_copy(o, i), "s_ve" if b == 1 else None)
            g["p12"] = evve
            g["sc", (0, 0)] = pr.later(g["sc", (0, 0)], evve)
            g["sc", (0, 1)] = pr.later(g["sc", (0, 1)], evve)
            vevac[t] = evve
            return last

        def ch_ev_sums(evd, n):
            evs = []
            for pp_ in range(n + 1):
                if pp_ in evd:
                    evs.extend([evd[pp_][0], evd[pp_][1]])
            return pr.later(*evs)

        def SC(qt, p, mask):
            bankT = P12 if p % 2 == 0 else P67
            evs = {}
            W("tensor", qevac[qt], kevac[qt - 1], kevac[qt], kevac[qt + 1],
              ev_c)
            W("tensor", g["sc", (p % 2, 0)], g["sc", (p % 2, 1)])
            i2 = qt % 2
            for hp in range(2):
                col0 = 512 * hp
                O("tensor", lambda E, o=bankT[:, col0:col0 + 2 * P],
                  l=ident[:, :], rh=mask[:, 0:2 * P]:
                  E.matmul(o, l, rh, start=True, stop=False))
            for j, kt in ((0, qt - 1), (1, qt + 1), (2, qt)):
                for hp in range(2):
                    col0 = 512 * hp
                    poff = 64 * hp
                    r = p
                    evs[hp] = O("tensor", lambda E,
                                o=bankT[:, col0 + P * j:col0 + P * (j + 1)],
                                l=q_sb[i2][poff:poff + HD, P * r:P * (r + 1)],
                                rh=k_sb[kt % 4][poff:poff + HD, P * r:P * (r + 1)],
                                sp=(j == 2):
                                E.matmul(o, l, rh, start=False, stop=sp),
                                "s_pe" if j == 2 else None)
            return pr.later(evs[0], evs[1])[0]

        def CH(qt, p, evsc_mm):
            bankT = P12 if p % 2 == 0 else P67
            evex = {}
            for hp in range(2):
                h = 2 * p + hp
                col0 = 512 * hp
                pidx = 2 * (p % 2) + hp
                W("vector", evsc_mm, g["nmax"][p % 2])
                evmx = O("vector", lambda E, o=nmax2[p % 2][:, hp:hp + 1],
                         i=bankT[:, col0:col0 + 384]: E.tensor_reduce(
                    o, i, axis=AX.X, op=ALU.max, negate=True), "s_ve")
                W("scalar", evmx, g["ssum"][h], g["probs_buf", pidx])
                evex[hp] = O("scalar", lambda E, o=probs[pidx][:, :],
                             i=bankT[:, col0:col0 + 384],
                             b=nmax2[p % 2][:, hp:hp + 1],
                             a=ssum[:, h:h + 1]: E.activation(
                    o, i, AF.Exp, bias=b, accum_out=a), "s_ae")
            g["nmax"][p % 2] = evex[1]
            g["sc", (p % 2, 0)] = pr.later(g["sc", (p % 2, 0)], evex[0])
            g["sc", (p % 2, 1)] = pr.later(g["sc", (p % 2, 1)], evex[1])
            return evex

        def TP_t(qt, p, evscale):
            evtt = {}
            W("tensor", evscale[0], g["t3"])
            for hp in range(2):
                if hp == 1:
                    W("tensor", evscale[1])
                pidx = 2 * (p % 2) + hp
                for j in range(3):
                    evtt[hp] = O("tensor", lambda E,
                                 o=T3[:, 384 * hp + P * j:384 * hp + P * (j + 1)],
                                 i=probs[pidx][:, P * j:P * (j + 1)], d=ident[:, :]:
                                 E.transpose(o, i, d), "s_pe" if j == 2 else None)
                g["probs_buf", pidx] = evtt[hp]
            return evtt

        def TP_pv(qt, p, evtt, pv_started):
            ring = 0
            if p % 2 == 0:
                W("scalar", evtt[0], evtt[1], g["probsT"][0], g["probsT"][1])
                evte = O("scalar", lambda E, o=probsT[:, ring:ring + 768],
                         i=T3[:, 0:768]: E.activation(o, i, AF.Copy), "s_ae")
            else:
                W("vector", evtt[0], evtt[1], g["probsT"][0], g["probsT"][1])
                evte = O("vector", lambda E, o=probsT[:, ring:ring + 768],
                         i=T3[:, 0:768]: E.tensor_copy(o, i), "s_ve")
            evteA = evteB = evte
            g["t3"] = evte
            import os as _os
            if _os.environ.get("K2_NO_PV"):
                g["pm"][2 * (p % 2)] = evte
                g["pm"][2 * (p % 2) + 1] = evte
                g["probsT"][0] = evte
                g["probsT"][1] = evte
                return evte
            W("tensor", evteA, evteB, vevac[qt - 1], vevac[qt], vevac[qt + 1])
            if p == 0:
                W("tensor", g["p45"])
            evpv = None
            for hp in range(2):
                h = 2 * p + hp
                poff = 64 * hp
                for j, kt in ((0, qt - 1), (1, qt + 1), (2, qt)):
                    evpv = O("tensor", lambda E,
                             o=P45[poff:poff + HD, P * p:P * (p + 1)],
                             l=v_sb[kt % 4][:, HD * h:HD * (h + 1)],
                             rh=probsT[:, ring + 384 * hp + P * j:
                                       ring + 384 * hp + P * (j + 1)],
                             st=(j == 0), sp=(j == 2), tp=(0, poff):
                             E.matmul(o, l, rh, start=st, stop=sp,
                                      tile_position=tp),
                             "s_pe" if (j == 2 and hp == 1) else None)
            g["probsT"][0] = evpv
            g["probsT"][1] = evpv
            return evpv

        def stage_B(qt, interleave_t=None, pre=None):
            import os as _os
            NPAIRS = int(_os.environ.get("K2_PAIRS", 8))
            NO_TP = bool(_os.environ.get("K2_NO_TP"))
            NO_TAIL = bool(_os.environ.get("K2_NO_TAIL"))
            i2, i3 = qt % 2, qt % 3
            mask = m_q2 if qt == 2 else (m_q17 if qt == 17 else m_mid)
            pv_started = set()
            evpv_by_p = {}
            sc_ev = {}
            evscale_by_p = {}
            if pre:
                sc_ev[0] = pre["sc0"]
                evscale_by_p[0] = pre["ch0"]
            else:
                sc_ev[0] = SC(qt, 0, mask)
            sc_ev[1] = SC(qt, 1, mask)
            if 0 not in evscale_by_p:
                evscale_by_p[0] = CH(qt, 0, sc_ev[0])
            if 1 not in evscale_by_p:
                evscale_by_p[1] = CH(qt, 1, sc_ev[1])
            for p in range(NPAIRS):
                evtt = TP_t(qt, p, evscale_by_p[p])
                if p + 2 <= NPAIRS - 1:
                    sc_ev[p + 2] = SC(qt, p + 2, mask)
                    evscale_by_p[p + 2] = CH(qt, p + 2, sc_ev[p + 2])
                if p == 0:
                    stage_C1b()
                if p == 1 and interleave_t is not None:
                    A1_T(interleave_t, a1_elem_ev[interleave_t])
                if NO_TP:
                    continue
                evpv_by_p[p] = TP_pv(qt, p, evtt, pv_started)
                if p == 5:
                    W("vector", ch_ev_sums(evscale_by_p, 5), g["rsumg"])
                    O("vector", lambda E, o=rsum[:, 0:12], i=ssum[:, 0:12]:
                      E.reciprocal(o, i), "s_ve")
                    evrc1 = ("s_ve", pr.cnt["s_ve"], "vector")
                    pr.wait("vector", (evrc1[0], evrc1[1], "_bar"))
                    W("vector", g["delta"])
                    rb1 = bass.AP(rsum, 0, [[NH, P], [2, 6], [1, 2], [0, 64]])
                    evrcb1 = O("vector", lambda E, o=delta[:, 0:768], i=rb1:
                               E.tensor_copy(o, i), "s_ve")
                    for h in range(12):
                        g["ssum"][h] = evrcb1
                    W("tensor", evrcb1, g["t3"])
                    evsm1 = None
                    for pp in range(6):
                        evsm1 = O("tensor", lambda E,
                                  o=T3[:, P * pp:P * (pp + 1)],
                                  i=delta[:, P * pp:P * (pp + 1)],
                                  d=ident[:, :]: E.transpose(o, i, d),
                                  "s_pe" if pp == 5 else None)
                    W("vector", evsm1)
                    evse1 = O("vector", lambda E, o=s_sb[:, 0:768],
                              i=T3[:, 0:768]: E.tensor_copy(o, i), "s_ve")
                    g["t3"] = evse1
                    g["rsumg"] = evse1
                    g["delta"] = pr.later(g["delta"], evsm1)
                    oA = outfm[:, 0:512].rearrange("a (b k) -> a b k", k=P)
                    pA = P45[:, 0:512].rearrange("a (b k) -> a b k", k=P)
                    sA = s_sb[:, 0:512].rearrange("a (b k) -> a b k", k=P)
                    W("vector", evpv_by_p[3], evse1, g["outfm"])
                    evoeA = O("vector", lambda E, o=oA, a=pA, b=sA:
                              E.tensor_tensor(o, a, b, op=ALU.mult), "s_ve")
                if p == 6 and interleave_t is not None:
                    g["p12"] = pr.later(g["p12"], g["sc", (0, 0)],
                                        g["sc", (0, 1)])
                    mm_v(interleave_t)
            if NO_TP or NPAIRS < 8 or NO_TAIL:
                scores_done[qt] = sc_ev[NPAIRS - 1]
                pv_done[qt] = evpv_by_p.get(NPAIRS - 1, sc_ev[NPAIRS - 1])
                g["k"][(qt - 1) % 4] = scores_done[qt]
                g["v"][(qt - 1) % 4] = pv_done[qt]
                g["q"][i2] = scores_done[qt]
                return None
            scores_done[qt] = sc_ev[7]
            pv_done[qt] = evpv_by_p[7]
            W("vector", g["sc", (1, 1)], g["rsumg"])
            O("vector", lambda E, o=rsum[:, 12:16], i=ssum[:, 12:16]:
              E.reciprocal(o, i), "s_ve")
            evrc = ("s_ve", pr.cnt["s_ve"], "vector")
            pr.wait("vector", (evrc[0], evrc[1], "_bar"))
            rbview = bass.AP(rsum, 12, [[NH, P], [2, 2], [1, 2], [0, 64]])
            evrcb = O("vector", lambda E, o=delta[:, 768:1024], i=rbview:
                      E.tensor_copy(o, i), "s_ve")
            for h in range(12, NH):
                g["ssum"][h] = evrcb
            W("tensor", evrcb, g["t3"])
            evsm = None
            for pp in range(6, NCH):
                evsm = O("tensor", lambda E, o=T3[:, P * pp:P * (pp + 1)],
                         i=delta[:, P * pp:P * (pp + 1)], d=ident[:, :]:
                         E.transpose(o, i, d), "s_pe" if pp == NCH - 1 else None)
            W("vector", evsm)
            evse = O("vector", lambda E, o=s_sb[:, 768:1024], i=T3[:, 768:1024]:
                     E.tensor_copy(o, i), "s_ve")
            g["t3"] = evse
            g["rsumg"] = evse
            g["delta"] = pr.later(g["delta"], evsm)
            oB = outfm[:, 512:1024].rearrange("a (b k) -> a b k", k=P)
            pB = P45[:, 512:1024].rearrange("a (b k) -> a b k", k=P)
            sB = s_sb[:, 512:1024].rearrange("a (b k) -> a b k", k=P)
            W("vector", evpv_by_p[7], evse, g["outfm"])
            evoeB = O("vector", lambda E, o=oB, a=pB, b=sB:
                      E.tensor_tensor(o, a, b, op=ALU.mult), "s_ve")
            g["p45"] = evoeB
            g["k"][(qt - 1) % 4] = scores_done[qt]
            g["v"][(qt - 1) % 4] = pv_done[qt]
            g["q"][i2] = scores_done[qt]

            W("tensor", evoeA, wdef["wo"], g["p67"],
              g["sc", (1, 0)], g["sc", (1, 1)])
            evap = None
            for c in range(NCH):
                if c == 4:
                    W("tensor", evoeB)
                for b in range(2):
                    evap = O("tensor", lambda E,
                             o=P67[:, 512 * b:512 * (b + 1)],
                             l=outfm[:, P * c:P * (c + 1)],
                             rh=wout[:, D * c + 512 * b:D * c + 512 * (b + 1)],
                             st=(c == 0), sp=(c == NCH - 1):
                             E.matmul(o, l, rh, start=st, stop=sp),
                             "s_pe" if (b == 1 and c == NCH - 1) else None)
            g["outfm"] = evap

            W("vector", evap, g["a2buf"], g["stat2"][i2])
            eva2 = O("vector", lambda E, o=a2buf[:, :], i=x_in[i3][:, :],
                     s=C(TWO), pp=P67[:, :], a=s2s[i2][:, :]:
                     E.scalar_tensor_tensor(o, i, s, pp, op0=ALU.mult,
                                            op1=ALU.add, accum_out=a), "s_ve")
            a2ev[qt] = eva2
            g["p67"] = eva2
            g["x_in"][i3] = pr.later(g["x_in"][i3], eva2)

            evsq, evr = stats(a2buf[:, :], s2s[i2], e22[i2], m2[i2], p2s[i2],
                              r2[i2], eva2, extra=g["stat2"][i2])
            W("scalar", evr)
            evmr = O("scalar", lambda E, o=lnvs[:, :], i=m2[i2][:, :],
                     s=r2[i2][:, :]: E.activation(o, i, AF.Copy, scale=s),
                     "s_ae")
            W("vector", evr, g["y_tm"][i2])
            evy = O("vector", lambda E, o=y_tm[i2][:, :], i=a2buf[:, :],
                    a=m2[i2][:, :], b=r2[i2][:, :]: E.tensor_scalar(
                o, i, a, b, op0=ALU.add, op1=ALU.mult), "s_ve")
            g["stat2"][i2] = evy
            W("scalar", evmr, g["ybf"])
            evyb = O("scalar", lambda E, o=ybf[:, :], i=a2buf[:, :],
                     s=r2[i2][:, :], b=lnvs[:, :]: E.activation(
                o, i, AF.Identity, scale=s, bias=b), "s_ae")
            g["a2buf"] = pr.later(evy, evsq, evyb)
            g["stat2"][i2] = pr.later(evy, evmr, evyb)
            return evyb

        def stage_B_tail(qt, evyb, interleave_t):
            if evyb is None:
                return
            i2 = qt % 2
            if interleave_t is not None:
                mm_q(interleave_t)
                mm_k(interleave_t)
            W("tensor", evyb, g["t0"])
            evyt = None
            for c in range(NCH):
                evyt = O("tensor", lambda E, o=T0[:, P * c:P * (c + 1)],
                         i=ybf[:, P * c:P * (c + 1)], d=ident[:, :]:
                         E.transpose(o, i, d), "s_pe" if c == NCH - 1 else None)
            g["ybf"] = evyt
            W("scalar", evyt, g["y_fm"][i2])
            yview = y_fm[i2][:, :].rearrange("a (c k) -> a c k", k=130)[:, :, 2:130]
            t0view = T0[:, :].rearrange("a (c k) -> a c k", k=P)
            evye = O("scalar", lambda E, o=yview, i=t0view:
                     E.activation(o, i, AF.Copy), "s_ae")
            yevac[qt] = evye
            g["t0"] = evye

        c2_pend = []
        c1b_pend = []

        def stage_C1b():
            if not c1b_pend:
                return
            ct, evr_, evks = c1b_pend.pop(0)
            i2 = ct % 2
            W("scalar", evr_, g["sig"])
            eve1 = O("scalar", lambda E, o=sig[:, :], i=P45[:, :], s=C(NEGONE):
                     E.activation(o, i, AF.Exp, scale=s), "s_ae")
            W("scalar", eve1, g["ae_scr"])
            evsp = O("scalar", lambda E, o=ae_scr[:, :], i=sig[:, :], b=C(ONE):
                     E.activation(o, i, AF.Ln, bias=b), "s_ae")
            W("scalar", evsp)
            evsg = O("scalar", lambda E, o=sig[:, :], i=ae_scr[:, :],
                     s=C(NEGONE): E.activation(o, i, AF.Exp, scale=s), "s_ae")
            g["ae_scr"] = evsg
            W("tensor", evks, wdef["wv"], eve1, g["p45"])
            evvo = None
            for b in range(2):
                for c in range(NCH):
                    evvo = O("tensor", lambda E,
                             o=P45[:, 512 * b:512 * (b + 1)],
                             l=kact[:, P * c:P * (c + 1)],
                             rh=wv[:, D * c + 512 * b:D * c + 512 * (b + 1)],
                             st=(c == 0), sp=(c == NCH - 1):
                             E.matmul(o, l, rh, start=st, stop=sp),
                             "s_pe" if (b == 1 and c == NCH - 1) else None)
            g["kact"] = evvo
            W("vector", evvo, evsg)
            O("vector", lambda E, o=sig[:, :], i=P45[:, :], s=C(ZERO):
              E.scalar_tensor_tensor(o, i, s, o, op0=ALU.add, op1=ALU.mult),
              "s_ve")
            evt1 = ("s_ve", pr.cnt["s_ve"], "vector")
            g["p45"] = evt1
            g["sig"] = evt1
            c2_pend.append((ct, evt1))

        def stage_C2():
            if not c2_pend:
                return
            ct, evt1 = c2_pend.pop(0)
            i2 = ct % 2
            W("vector", evt1, g["stat3"][i2])
            eva3 = O("vector", lambda E, o=sig[:, :], i=sig[:, :], s=C(ZERO),
                     t=y_tm[i2][:, :], a=s3s[i2][:, :]:
                     E.scalar_tensor_tensor(o, i, s, t, op0=ALU.add,
                                            op1=ALU.add, accum_out=a), "s_ve")
            g["sig"] = eva3
            g["y_tm"][i2] = pr.later(g["y_tm"][i2], eva3)
            evsq, evr = stats(sig[:, :], s3s[i2], e23[i2], m3[i2], p3s[i2],
                              r3[i2], eva3, extra=g["stat3"][i2])
            W("vector", evr, g["out_sb"][i2])
            evout = O("vector", lambda E, o=out_sb[i2][:, :], i=sig[:, :],
                      a=m3[i2][:, :], b=r3[i2][:, :]: E.tensor_scalar(
                o, i, a, b, op0=ALU.add, op1=ALU.mult), "s_ve")
            g["stat3"][i2] = evout
            g["sig"] = pr.later(g["sig"], evout, evsq)
            W("gpsimd", evout)
            evst = dma("gpsimd", f"s_st{i2}",
                       out_d[P * (ct - CT0):P * (ct - CT0 + 1), :],
                       out_sb[i2][:, :])
            g["out_sb"][i2] = evst

        def stage_C(ct):
            i2, pp2 = ct % 2, (ct - 1) % 2

            W("vector", yevac[ct], yevac[ct - 1], g["y_fm"][i2], ev_cv, ev_c)
            evpc = None
            for c in range(NCH):
                if ct == CT0:
                    evpc = O("vector", lambda E,
                             o=y_fm[i2][:, 130 * c + 1:130 * c + 2],
                             i=y_fm[pp2][:, 130 * c + 129:130 * c + 130],
                             s=gate[:, :]: E.tensor_scalar(
                        o, i, s, None, op0=ALU.mult),
                        "s_ve" if c == NCH - 1 else None)
                else:
                    evpc = O("vector", lambda E,
                             o=y_fm[i2][:, 130 * c + 1:130 * c + 2],
                             i=y_fm[pp2][:, 130 * c + 129:130 * c + 130]:
                             E.tensor_copy(o, i), "s_ve" if c == NCH - 1 else None)
            g["y_fm"][pp2] = pr.later(g["y_fm"][pp2], evpc)

            own = y_fm[i2][:, :].rearrange("a (c k) -> a c k", k=130)[:, :, 2:130]
            prev = y_fm[i2][:, :].rearrange("a (c k) -> a c k", k=130)[:, :, 1:129]
            dview = delta[:, :].rearrange("a (c k) -> a c k", k=P)

            W("vector", evpc, g["delta"])
            evd = O("vector", lambda E, o=dview, a=own, b=prev:
                    E.tensor_tensor(o, a, b, op=ALU.subtract), "s_ve")
            W("vector", evd, g["xr"])
            evxr = None
            for c in range(NCH):
                evxr = O("vector", lambda E, o=xr_fm[:, P * c:P * (c + 1)],
                         i=delta[:, P * c:P * (c + 1)], s=mur[:, c:c + 1],
                         t=y_fm[i2][:, 130 * c + 1:130 * c + 129]:
                         E.scalar_tensor_tensor(o, i, s, t, op0=ALU.mult,
                                                op1=ALU.add),
                         "s_ve" if c == NCH - 1 else None)
            W("vector", evd, g["xk"])
            evxk = None
            for c in range(NCH):
                evxk = O("vector", lambda E, o=xk_fm[:, P * c:P * (c + 1)],
                         i=delta[:, P * c:P * (c + 1)], s=muk[:, c:c + 1],
                         t=y_fm[i2][:, 130 * c + 1:130 * c + 129]:
                         E.scalar_tensor_tensor(o, i, s, t, op0=ALU.mult,
                                                op1=ALU.add),
                         "s_ve" if c == NCH - 1 else None)
            g["delta"] = pr.later(evxr, evxk)
            g["y_fm"][i2] = pr.later(g["y_fm"][i2], evxr, evxk)

            W("tensor", evxr, wdef["wr"], g["p45"])
            evr_ = None
            for b in range(2):
                for c in range(NCH):
                    evr_ = O("tensor", lambda E,
                             o=P45[:, 512 * b:512 * (b + 1)],
                             l=xr_fm[:, P * c:P * (c + 1)],
                             rh=wr[:, D * c + 512 * b:D * c + 512 * (b + 1)],
                             st=(c == 0), sp=(c == NCH - 1):
                             E.matmul(o, l, rh, start=st, stop=sp),
                             "s_pe" if (b == 1 and c == NCH - 1) else None)
            g["xr"] = evr_

            W("tensor", evxk, wdef["wk"], g["p67"])
            evkf = None
            for r_ in range(NCH):
                for c in range(NCH):
                    evkf = O("tensor", lambda E,
                             o=P67[:, P * r_:P * (r_ + 1)],
                             l=wk[:, D * c + P * r_:D * c + P * (r_ + 1)],
                             rh=xk_fm[:, P * c:P * (c + 1)],
                             st=(r_ % 4 == 0 and c == 0),
                             sp=(r_ % 4 == 3 and c == NCH - 1):
                             E.matmul(o, l, rh, start=st, stop=sp),
                             "s_pe" if (r_ == NCH - 1 and c == NCH - 1) else None)
            g["xk"] = evkf

            W("scalar", evkf, g["krelu"], g["delta"])
            evkr = O("scalar", lambda E, o=krelu[:, :], i=P67[:, :]:
                     E.activation(o, i, AF.Relu), "s_ae")
            g["p67"] = evkr
            g["sc", (1, 0)] = evkr
            g["sc", (1, 1)] = evkr
            W("vector", evkr, g["kact"])
            evks = O("vector", lambda E, o=kact[:, :], i=krelu[:, :]:
                     E.tensor_mul(o, i, i), "s_ve")
            g["krelu"] = evks
            g["delta"] = pr.later(g["delta"], evks)

            c1b_pend.append((ct, evr_, evks))

        a1_elem_ev = {}
        for t in range(3):
            a1_elem_ev[t] = A1_elem(t)
            A1_T(t, a1_elem_ev[t])
            if QT0 <= t <= QT1:
                mm_q(t)
            mm_k(t)
            mm_v(t)
        for c in range(NCH):
            wdef["wo"] = dma("gpsimd", "s_wo", wout[:, D * c:D * (c + 1)],
                             wout_d[P * c:P * (c + 1), :])
        for c in range(NCH):
            wdef["wr"] = dma("gpsimd", "s_wr", wr[:, D * c:D * (c + 1)],
                             wr_d[P * c:P * (c + 1), :])
        for c in range(NCH):
            wdef["wk"] = dma("gpsimd", "s_wk", wk[:, D * c:D * (c + 1)],
                             wk_d[P * c:P * (c + 1), :])
        for c in range(NCH):
            wdef["wv"] = dma("gpsimd", "s_wv", wv[:, D * c:D * (c + 1)],
                             wv_d[P * c:P * (c + 1), :])

        import os as _os
        ILV = not _os.environ.get("K2_NO_ILV")
        MAXT = int(_os.environ.get("K2_MAXT", NT - 1))
        pre_b = None
        for t in range(1, MAXT):
            nxt = t + 2 if t + 2 <= NT - 1 else None
            if nxt is not None:
                W("sync", g["x_in"][nxt % 3])
                ld_ev[nxt] = dma("sync", f"s_x{nxt % 3}", x_in[nxt % 3][:, :],
                                 x_d[P * nxt:P * (nxt + 1), :])
                a1_elem_ev[nxt] = A1_elem(nxt)
            if not ILV and nxt is not None:
                A1_T(nxt, a1_elem_ev[nxt])
                mm_v(nxt)
                mm_q(nxt)
                mm_k(nxt)
            evyb = stage_B(t, interleave_t=(nxt if ILV else None), pre=pre_b)
            stage_C2()
            stage_B_tail(t, evyb, (nxt if ILV else None))
            pre_b = None
            if t + 1 < MAXT:
                mask_n = m_q2 if t + 1 == 2 else (m_q17 if t + 1 == 17 else m_mid)
                sc0 = SC(t + 1, 0, mask_n)
                ch0 = CH(t + 1, 0, sc0)
                pre_b = {"sc0": sc0, "ch0": ch0}
            if t >= CT0:
                stage_C(t)

        stage_C1b()
        stage_C2()
        stage_C2()
        if pr.cnt.get("s_st0"):
            W("gpsimd", ("s_st0", pr.cnt["s_st0"], "_dma"))
        if pr.cnt.get("s_st1"):
            W("gpsimd", ("s_st1", pr.cnt["s_st1"], "_dma"))

        with nc.Block() as block:
            def make_runner(name):
                def run(E):
                    for item in pr.ops[name]:
                        if item[0] == "w":
                            E.wait_ge(sems[item[1]], item[2])
                        else:
                            ins = item[1](E)
                            if item[2] is not None:
                                ins.then_inc(sems[item[2]], item[3])
                return run

            block.sync(make_runner("sync"))
            block.vector(make_runner("vector"))
            block.scalar(make_runner("scalar"))
            block.tensor(make_runner("tensor"))
            block.gpsimd(make_runner("gpsimd"))

    return nc


_CACHE = {}


def _get_nc():
    if "nc" not in _CACHE:
        _CACHE["nc"] = build_program()
    return _CACHE["nc"]


def _build_masks(h):
    i = np.arange(P)[:, None]
    j = np.arange(3 * P)[None, :]
    band = (j - i >= 0) & (j - i <= 256)
    NEG = -1e9
    mid = np.where(band, 0.0, NEG).astype(np.float32)
    q2 = np.where(band & (j >= 128), 0.0, NEG).astype(np.float32) if h == 0 else mid
    q17 = np.where(band & (j < 256), 0.0, NEG).astype(np.float32) if h == 1 else mid
    perm = np.r_[0:128, 256:384, 128:256]
    bf = ml_dtypes.bfloat16
    return (mid[:, perm].astype(bf), q2[:, perm].astype(bf),
            q17[:, perm].astype(bf))


def kernel(L0_attn_x, L0_attn_w_qkv, L0_attn_w_out, L0_attn_gamma, L0_attn_beta,
           L0_ffn_W_r, L0_ffn_W_k, L0_ffn_W_v, L0_ffn_mu_r, L0_ffn_mu_k):
    x = np.asarray(L0_attn_x, dtype=np.float32)
    wqkvT = np.ascontiguousarray(np.asarray(L0_attn_w_qkv, dtype=np.float32).T)
    woutT = np.ascontiguousarray(np.asarray(L0_attn_w_out, dtype=np.float32).T)
    wrT = np.ascontiguousarray(np.asarray(L0_ffn_W_r, dtype=np.float32).T)
    wkT = np.ascontiguousarray(np.asarray(L0_ffn_W_k, dtype=np.float32).T)
    wvT = np.ascontiguousarray(np.asarray(L0_ffn_W_v, dtype=np.float32).T)

    def fm(vec):
        return np.ascontiguousarray(
            np.asarray(vec, dtype=np.float32).reshape(NCH, P).T)

    gamma_fm, beta_fm = fm(L0_attn_gamma), fm(L0_attn_beta)
    mur_fm, muk_fm = fm(L0_ffn_mu_r), fm(L0_ffn_mu_k)
    ident = np.eye(P, dtype=np.float32).astype(ml_dtypes.bfloat16)

    in_maps = []
    for core in range(8):
        b, h = core // 2, core % 2
        s = h * 2048
        xp = np.zeros((NT * P, D), np.float32)
        lo, hi = s - 256, s + 2176
        clo, chi = max(lo, 0), min(hi, 4096)
        xp[clo - lo:chi - lo] = x[b, clo:chi]
        mid, q2, q17 = _build_masks(h)
        in_maps.append({
            "x": xp, "wqkvT": wqkvT, "woutT": woutT, "wrT": wrT, "wkT": wkT,
            "wvT": wvT, "gamma_fm": gamma_fm, "beta_fm": beta_fm,
            "mur_fm": mur_fm, "muk_fm": muk_fm,
            "gate": np.full((P, 1), 0.0 if h == 0 else 1.0, np.float32),
            "ident": ident, "mask_mid": mid, "mask_q2": q2, "mask_q17": q17,
        })

    nc = _get_nc()
    res = run_bass_kernel_spmd(nc, in_maps, core_ids=list(range(8)))
    out = np.empty((4, 4096, D), np.float32)
    for core in range(8):
        b, h = core // 2, core % 2
        out[b, h * 2048:(h + 1) * 2048] = res.results[core]["out"]
    return out
